# version 06
# baseline (speedup 1.0000x reference)
import numpy as np
import ml_dtypes
from contextlib import ExitStack

import concourse.bass as bass
import concourse.mybir as mybir
from concourse.bass_utils import run_bass_kernel_spmd

F32 = mybir.dt.float32
BF16 = mybir.dt.bfloat16
AF = mybir.ActivationFunctionType
ALU = mybir.AluOpType
AX = mybir.AxisListType

T = 2048
NT = 16
D = 1024
ND = 8
NH = 4
DH = 256
J = 512
F = 4096
E = 8
FH = 2048
NFC = 16

ENGS = ("sync", "tensor", "scalar", "vector", "gpsimd")


def build_nc(dbg=0):
    nc = bass.Bass("TRN2", target_bir_lowering=False)

    x_d = nc.dram_tensor("x", [T, D], F32, kind="ExternalInput").ap()
    wqt_d = nc.dram_tensor("wqt", [128, ND * D], F32, kind="ExternalInput").ap()
    w0t_d = nc.dram_tensor("w0t", [128, 4096], F32, kind="ExternalInput").ap()
    w2t_d = nc.dram_tensor("w2t", [128, 4096], F32, kind="ExternalInput").ap()
    w1t_d = nc.dram_tensor("w1t", [128, 4096], F32, kind="ExternalInput").ap()
    wred_d = nc.dram_tensor("wred", [128, 128], F32, kind="ExternalInput").ap()
    wg_d = nc.dram_tensor("wg", [8, 16], F32, kind="ExternalInput").ap()
    qnw_d = nc.dram_tensor("qnw", [128, D], F32, kind="ExternalInput").ap()
    b1c_d = nc.dram_tensor("b1c", [128, 256], F32, kind="ExternalInput").ap()
    b2r_d = nc.dram_tensor("b2r", [1, E * D], F32, kind="ExternalInput").ap()
    ident_d = nc.dram_tensor("ident", [128, 128], BF16, kind="ExternalInput").ap()
    ones_d = nc.dram_tensor("ones1", [1, 128], BF16, kind="ExternalInput").ap()
    w1_d = nc.dram_tensor("w1e", [E, D, F], F32, kind="ExternalInput").ap()
    w2_d = nc.dram_tensor("w2e", [E, F, D], F32, kind="ExternalInput").ap()
    out_d = nc.dram_tensor("out", [T, D], F32, kind="ExternalOutput").ap()
    if dbg:
        dbgQT_d = nc.dram_tensor("dbgQT", [128, ND * T], BF16, kind="ExternalOutput").ap()
        dbgX1T_d = nc.dram_tensor("dbgX1T", [128, ND * T], BF16, kind="ExternalOutput").ap()
        dbgC_d = nc.dram_tensor("dbgC", [128, NT * E], F32, kind="ExternalOutput").ap()
        dbgRED_d = nc.dram_tensor("dbgRED", [16, T], BF16, kind="ExternalOutput").ap()

    blocks = []

    def new_block():
        b = {e: [] for e in ENGS}
        blocks.append(b)
        return b

    OPS = []
    sem_hist = {}
    sem_cum = {}

    def emit(blk, eng, make, waits=(), inc=None, inc_by=1):
        wl = [(h, v) for (h, v) in waits]

        def fn(e):
            for (h, v) in wl:
                e.wait_ge(h, v)
            ins = make(e)
            if inc is not None:
                ins.then_inc(inc, inc_by)
        blk[eng].append(fn)
        oi = len(OPS)
        rec = dict(eng=eng, waits=[(id(h), v) for (h, v) in wl], inc=None, oi=oi)
        if inc is not None:
            k = id(inc)
            sem_cum[k] = sem_cum.get(k, 0) + inc_by
            sem_hist.setdefault(k, []).append((sem_cum[k], oi))
            rec["inc"] = (k, sem_cum[k])
        OPS.append(rec)

    def check_deadlock():
        n = len(OPS)
        adj = [[] for _ in range(n)]
        indeg = [0] * n
        last_on = {}
        for op in OPS:
            oi = op["oi"]
            if op["eng"] in last_on:
                adj[last_on[op["eng"]]].append(oi)
                indeg[oi] += 1
            last_on[op["eng"]] = oi
            for (k, v) in op["waits"]:
                hist = sem_hist.get(k, [])
                ach = None
                for (cum, j) in hist:
                    if cum >= v:
                        ach = j
                        break
                if ach is None:
                    raise AssertionError(f"op {oi} ({op['eng']}) waits sem value {v} never reached")
                adj[ach].append(oi)
                indeg[oi] += 1
        from collections import deque
        dq = deque(i for i in range(n) if indeg[i] == 0)
        seen = 0
        while dq:
            u = dq.popleft()
            seen += 1
            for v2 in adj[u]:
                indeg[v2] -= 1
                if indeg[v2] == 0:
                    dq.append(v2)
        if seen != n:
            stuck = [i for i in range(n) if indeg[i] > 0][:12]
            detail = [(i, OPS[i]["eng"], OPS[i]["waits"]) for i in stuck]
            raise AssertionError(f"deadlock cycle among ops (first 12): {detail}")

    class C:
        tp = 0; sp = 0; vp = 0
        s_x = 0; s_small = 0; s_base = 0; s_wq = 0; s_wh = 0; s_b2 = 0
        s_w1 = [0, 0]; s_w2 = [0, 0]; s_acc = [0, 0]

    with ExitStack() as st:
        en = st.enter_context
        TP = en(nc.semaphore("TP")); SP = en(nc.semaphore("SP")); VP = en(nc.semaphore("VP"))
        SX = en(nc.semaphore("SX")); SSM = en(nc.semaphore("SSM")); SB = en(nc.semaphore("SBASE"))
        SWQ = en(nc.semaphore("SWQ")); SWH = en(nc.semaphore("SWH")); SB2 = en(nc.semaphore("SB2"))
        SW1 = [en(nc.semaphore("SW1a")), en(nc.semaphore("SW1b"))]
        SW2 = [en(nc.semaphore("SW2a")), en(nc.semaphore("SW2b"))]
        SAC = [en(nc.semaphore("SACa")), en(nc.semaphore("SACb"))]
        SDBG = en(nc.semaphore("SDBG"))

        X1T = en(nc.sbuf_tensor("X1T", [128, ND * T], BF16))
        c_sb = en(nc.sbuf_tensor("c_sb", [128, NT * E], F32))
        ident = en(nc.sbuf_tensor("identb", [128, 128], BF16))
        ones1 = en(nc.sbuf_tensor("ones1b", [1, 128], BF16))
        b1c = en(nc.sbuf_tensor("b1cb", [128, 256], F32))
        wg_sb = en(nc.sbuf_tensor("wg_sb", [8, 16], F32))
        wred_bf = en(nc.sbuf_tensor("wred_bf", [128, 128], BF16))
        wgnT = en(nc.sbuf_tensor("wgnT", [16, 8], BF16))
        cols = en(nc.sbuf_tensor("cols", [128, 24], F32))
        lgS = en(nc.sbuf_tensor("lgS", [128, 16], F32))
        routS = en(nc.sbuf_tensor("routS", [128, 24], F32))

        (S_SUM, S_SSQ, S_M, S_E2, S_MSQ, S_VAR, S_SD, S_RS, S_NB,
         S_SS2, S_SD2, S_RS2, S_NMX, S_ESUM, S_REC, S_REC2, S_SD2B) = range(17)

        def col(i):
            return cols[:, i:i + 1]

        def ln_stats(B, src_sb, off, dst_stage, eps, w_scalar=(), w_vector=()):
            xsl = src_sb[:, off:off + D]
            C.sp += 1
            emit(B, "scalar", (lambda xs: (lambda e: e.activation(sqscr[:], xs, AF.Square)))(xsl),
                 waits=list(w_scalar), inc=SP)
            sp_sq = C.sp
            C.vp += 1
            emit(B, "vector", (lambda xs: (lambda e: e.reduce_sum(col(S_SUM), xs, axis=AX.X)))(xsl),
                 waits=list(w_vector), inc=VP)
            v_sum = C.vp
            C.vp += 1
            emit(B, "vector", lambda e: e.reduce_sum(col(S_SSQ), sqscr[:], axis=AX.X),
                 waits=[(SP, sp_sq)], inc=VP)
            v_ssq = C.vp
            C.vp += 1
            emit(B, "vector", lambda e: e.tensor_scalar(col(S_M), col(S_SUM), 1.0 / D, 0.0, ALU.mult, ALU.add),
                 waits=[(VP, v_sum)], inc=VP)
            v_m = C.vp
            C.vp += 1
            emit(B, "vector", (lambda ep: (lambda e: e.tensor_scalar(col(S_E2), col(S_SSQ), 1.0 / D, ep, ALU.mult, ALU.add)))(eps),
                 waits=[(VP, v_ssq)], inc=VP)
            C.vp += 1
            emit(B, "vector", lambda e: e.tensor_mul(col(S_MSQ), col(S_M), col(S_M)),
                 waits=[(VP, v_m)], inc=VP)
            v_msq = C.vp
            C.vp += 1
            emit(B, "vector", lambda e: e.tensor_scalar(col(S_VAR), col(S_MSQ), -1.0, col(S_E2), ALU.mult, ALU.add),
                 waits=[(VP, v_msq)], inc=VP)
            vp_var = C.vp
            C.sp += 1
            emit(B, "scalar", lambda e: e.activation(col(S_SD), col(S_VAR), AF.Sqrt),
                 waits=[(VP, vp_var)], inc=SP)
            sp_sd = C.sp
            C.vp += 1
            emit(B, "vector", lambda e: e.reciprocal(col(S_RS), col(S_SD)), waits=[(SP, sp_sd)], inc=VP)
            v_rs = C.vp
            C.vp += 1
            emit(B, "vector", lambda e: e.tensor_scalar(col(S_NB), col(S_M), col(S_RS), -1.0, ALU.mult, ALU.mult),
                 waits=[(VP, v_rs)], inc=VP)
            vp_nb = C.vp
            C.sp += 1
            emit(B, "scalar", (lambda xs, dst: (lambda e: e.activation(dst[:], xs, AF.Identity, bias=col(S_NB), scale=col(S_RS))))(xsl, dst_stage),
                 waits=[(VP, vp_nb)], inc=SP)
            return C.sp, v_ssq

        with ExitStack() as sxs:
            ex_ = sxs.enter_context
            x_sb = ex_(nc.sbuf_tensor("x_sb", [128, NT * D], F32))
            sqscr = ex_(nc.sbuf_tensor("sqscr", [128, D], F32))

            with ExitStack() as sqt:
                eq_ = sqt.enter_context
                QT = eq_(nc.sbuf_tensor("QT", [128, ND * T], BF16))

                with ExitStack() as s1:
                    e1 = s1.enter_context
                    WqT = e1(nc.sbuf_tensor("WqT", [128, ND * D], BF16))
                    qnw = e1(nc.sbuf_tensor("qnw_sb", [128, D], F32))
                    xln_st = [e1(nc.sbuf_tensor(f"xln{i}", [128, D], BF16)) for i in range(2)]
                    xt0_st = [e1(nc.sbuf_tensor(f"xt0{i}", [128, D], BF16)) for i in range(2)]
                    qn_st = [e1(nc.sbuf_tensor(f"qn{i}", [128, D], BF16)) for i in range(2)]
                    qw_sb = e1(nc.sbuf_tensor("qw", [128, D], F32))
                    qw2 = e1(nc.sbuf_tensor("qw2", [128, D], F32))
                    h4 = e1(nc.sbuf_tensor("h4", [128, 16], F32))
                    psT = [e1(nc.psum_tensor(f"psT{i}", [128, 1024], BF16)) for i in range(2)]
                    psT2 = [e1(nc.psum_tensor(f"psT2{i}", [128, 1024], BF16)) for i in range(2)]
                    psQ = [e1(nc.psum_tensor(f"psQ{i}", [128, 1024], F32)) for i in range(2)]

                    B1 = new_block()

                    for g in range(4):
                        src = x_d[g * 512:(g + 1) * 512, :].rearrange("(n p) d -> p n d", p=128)
                        dst = x_sb[:, g * 4096:(g + 1) * 4096].rearrange("p (n d) -> p n d", n=4)
                        C.s_x += 16
                        emit(B1, "sync", (lambda d_, s_: (lambda e: e.dma_start(d_, s_)))(dst, src),
                             inc=SX, inc_by=16)
                    for (dst, src) in ((qnw[:], qnw_d), (b1c[:], b1c_d), (wg_sb[:], wg_d),
                                       (ident[:], ident_d), (ones1[:], ones_d)):
                        C.s_small += 16
                        emit(B1, "sync", (lambda d_, s_: (lambda e: e.dma_start(d_, s_)))(dst, src),
                             inc=SSM, inc_by=16)
                    C.s_wq += 16
                    emit(B1, "gpsimd", lambda e: e.dma_start(WqT[:], wqt_d), inc=SWQ, inc_by=16)
                    C.s_wq += 16
                    emit(B1, "gpsimd", lambda e: e.dma_start(wred_bf[:], wred_d), inc=SWQ, inc_by=16)

                    xs_done = [0] * NT; q_done = [0] * NT; tq_done = [0] * NT
                    qt_evac = [0] * NT; qn_done = [0] * NT; qcons_v = [0] * NT
                    hs_done = [0] * NT; ssq_done = [0] * NT

                    for t in range(NT):
                        a = t % 2
                        xw = [(SX, 64)]
                        xws = list(xw)
                        if t >= 1:
                            xws.append((VP, ssq_done[t - 1]))
                        ln_sp, ssq_done[t] = ln_stats(B1, x_sb, t * D, xln_st[a], 1e-5, w_scalar=xws, w_vector=xw)
                        w = [(SP, ln_sp), (SSM, C.s_small)]
                        if t >= 2:
                            w.append((SP, xs_done[t - 2]))
                        for k in range(ND):
                            emit(B1, "tensor",
                                 (lambda aa, kk: (lambda e: e.transpose(psT[aa][:, kk * 128:(kk + 1) * 128],
                                                                        xln_st[aa][:, kk * 128:(kk + 1) * 128], ident[:])))(a, k),
                                 waits=(w if k == 0 else ()), inc=(TP if k == ND - 1 else None))
                        C.tp += 1
                        tx_tp = C.tp
                        C.sp += 1
                        emit(B1, "scalar", (lambda aa: (lambda e: e.copy(xt0_st[aa][:], psT[aa][:])))(a),
                             waits=[(TP, tx_tp)], inc=SP)
                        xs_done[t] = C.sp
                        w = [(SP, xs_done[t]), (SWQ, 32)]
                        if t >= 2:
                            w.append((VP, qcons_v[t - 2]))
                        for k in range(ND):
                            for hlf in range(2):
                                last = (k == ND - 1 and hlf == 1)
                                emit(B1, "tensor",
                                     (lambda aa, kk, hh, st_, so_: (lambda e: e.matmul(
                                         psQ[aa][:, hh * 512:(hh + 1) * 512],
                                         xt0_st[aa][:, kk * 128:(kk + 1) * 128],
                                         WqT[:, kk * D + hh * 512: kk * D + (hh + 1) * 512],
                                         start=st_, stop=so_)))(a, k, hlf, k == 0, k == ND - 1),
                                     waits=(w if (k == 0 and hlf == 0) else ()),
                                     inc=(TP if last else None))
                        C.tp += 1
                        q_done[t] = C.tp
                        wq2 = [(TP, q_done[t])]
                        if t >= 1:
                            wq2.append((VP, hs_done[t - 1]))
                        C.sp += 1
                        emit(B1, "scalar", (lambda aa: (lambda e: e.activation(qw2[:], psQ[aa][:], AF.Square)))(a),
                             waits=wq2, inc=SP)
                        sp_q2 = C.sp
                        C.vp += 1
                        emit(B1, "vector", lambda e: e.reduce_sum(col(S_SS2), qw2[:], axis=AX.X),
                             waits=[(SP, sp_q2)], inc=VP)
                        v_ss2 = C.vp
                        C.vp += 1
                        emit(B1, "vector", lambda e: e.tensor_scalar(col(S_SD2), col(S_SS2), 1.0 / D, 1e-6, ALU.mult, ALU.add),
                             waits=[(VP, v_ss2)], inc=VP)
                        vp_sd2p = C.vp
                        C.sp += 1
                        emit(B1, "scalar", lambda e: e.activation(col(S_SD2B), col(S_SD2), AF.Sqrt),
                             waits=[(VP, vp_sd2p)], inc=SP)
                        sp_sd2 = C.sp
                        C.vp += 1
                        emit(B1, "vector", (lambda aa: (lambda e: e.tensor_mul(qw_sb[:], psQ[aa][:], qnw[:])))(a),
                             waits=[(TP, q_done[t]), (SSM, C.s_small)], inc=VP)
                        v_qw = C.vp
                        C.vp += 1
                        emit(B1, "vector", lambda e: e.reciprocal(col(S_RS2), col(S_SD2B)), waits=[(SP, sp_sd2)], inc=VP)
                        v_rs2 = C.vp
                        C.vp += 1
                        emit(B1, "vector", lambda e: e.tensor_mul(qw2[:], qw_sb[:], qw_sb[:]),
                             waits=[(VP, v_qw), (SP, sp_q2)], inc=VP)
                        v_qw2 = C.vp
                        for h in range(NH):
                            C.vp += 1
                            emit(B1, "vector",
                                 (lambda hh: (lambda e: e.reduce_sum(h4[:, hh:hh + 1], qw2[:, hh * DH:(hh + 1) * DH], axis=AX.X)))(h),
                                 waits=([(VP, v_qw2)] if h == 0 else ()), inc=VP)
                        vp_hs = C.vp
                        hs_done[t] = C.vp
                        C.sp += 1
                        emit(B1, "scalar", lambda e: e.activation(h4[:, 4:8], h4[:, 0:4], AF.Sqrt),
                             waits=[(VP, vp_hs)], inc=SP)
                        sp_hsr = C.sp
                        C.vp += 1
                        emit(B1, "vector", lambda e: e.tensor_scalar(h4[:, 8:12], h4[:, 4:8], col(S_RS2), 1e-5, ALU.mult, ALU.add),
                             waits=[(SP, sp_hsr), (VP, v_rs2)], inc=VP)
                        v_t4 = C.vp
                        C.vp += 1
                        emit(B1, "vector", lambda e: e.reciprocal(h4[:, 12:16], h4[:, 8:12]),
                             waits=[(VP, v_t4)], inc=VP)
                        v_t5 = C.vp
                        C.vp += 1
                        emit(B1, "vector", lambda e: e.tensor_scalar(h4[:, 12:16], h4[:, 12:16], col(S_RS2), 0.0, ALU.mult, ALU.add),
                             waits=[(VP, v_t5)], inc=VP)
                        qcons_v[t] = C.vp
                        vp_fs = C.vp
                        for h in range(NH):
                            C.sp += 1
                            emit(B1, "scalar",
                                 (lambda aa, hh: (lambda e: e.activation(
                                     qn_st[aa][:, hh * DH:(hh + 1) * DH], qw_sb[:, hh * DH:(hh + 1) * DH],
                                     AF.Identity, scale=h4[:, 12 + hh:13 + hh])))(a, h),
                                 waits=([(VP, vp_fs)] if h == 0 else ()), inc=SP)
                        qn_done[t] = C.sp
                        w = [(SP, qn_done[t])]
                        if t >= 2:
                            w.append((VP, qt_evac[t - 2]))
                        for k in range(ND):
                            emit(B1, "tensor",
                                 (lambda aa, kk: (lambda e: e.transpose(psT2[aa][:, kk * 128:(kk + 1) * 128],
                                                                        qn_st[aa][:, kk * 128:(kk + 1) * 128], ident[:])))(a, k),
                                 waits=(w if k == 0 else ()), inc=(TP if k == ND - 1 else None))
                        C.tp += 1
                        tq_done[t] = C.tp
                        C.vp += 1
                        emit(B1, "vector",
                             (lambda aa, tt: (lambda e: e.tensor_copy(
                                 QT[:].rearrange("p (i c) -> p i c", i=ND)[:, :, tt * 128:(tt + 1) * 128],
                                 psT2[aa][:].rearrange("p (i c) -> p i c", i=ND))))(a, t),
                             waits=[(TP, tq_done[t])], inc=VP)
                        qt_evac[t] = C.vp

                    b1_tp, b1_sp, b1_vp = C.tp, C.sp, C.vp

                with ExitStack() as s2:
                    e2 = s2.enter_context
                    w0T = e2(nc.sbuf_tensor("w0T", [128, 4096], BF16))
                    w2T = e2(nc.sbuf_tensor("w2T", [128, 4096], BF16))
                    w1T = e2(nc.sbuf_tensor("w1T", [128, 4096], BF16))
                    gq_st = [e2(nc.sbuf_tensor(f"gq{i}", [128, 512], F32)) for i in range(2)]
                    G_sb = e2(nc.sbuf_tensor("G_sb", [128, 4 * T], BF16))
                    psG = [e2(nc.psum_tensor(f"psG{i}", [128, 512], F32)) for i in range(2)]
                    psO = [e2(nc.psum_tensor(f"psO{i}", [128, 512], F32)) for i in range(2)]

                    B2 = new_block()
                    barA = [(TP, b1_tp), (SP, b1_sp), (VP, b1_vp)]
                    for i, (dst, src) in enumerate(((w0T, w0t_d), (w2T, w2t_d), (w1T, w1t_d))):
                        C.s_wh += 16
                        emit(B2, "gpsimd", (lambda d_, s_: (lambda e: e.dma_start(d_[:], s_)))(dst, src),
                             waits=(barA if i == 0 else ()), inc=SWH, inc_by=16)

                    sil_done = {}; g_done = {}
                    oev_done = [0] * NT
                    for h in range(NH):
                        for jt in range(4):
                            for tb in range(4):
                                i = h * 16 + jt * 4 + tb
                                w = [(SWH, 48), (VP, qt_evac[NT - 1])]
                                if i >= 1:
                                    w.append((SP, sil_done[i - 1]))
                                for dhc in range(2):
                                    emit(B2, "tensor",
                                         (lambda hh, jj, bb, dd, st_, so_: (lambda e: e.matmul(
                                             psG[0][:],
                                             w0T[:, hh * 1024 + dd * 512 + jj * 128: hh * 1024 + dd * 512 + (jj + 1) * 128],
                                             QT[:, (2 * hh + dd) * T + bb * 512: (2 * hh + dd) * T + (bb + 1) * 512],
                                             start=st_, stop=so_)))(h, jt, tb, dhc, dhc == 0, dhc == 1),
                                         waits=(w if dhc == 0 else ()), inc=(TP if dhc == 1 else None))
                                C.tp += 1
                                tp_gq = C.tp
                                w2_ = []
                                if i >= 1:
                                    w2_.append((VP, g_done[i - 1]))
                                for dhc in range(2):
                                    emit(B2, "tensor",
                                         (lambda hh, jj, bb, dd, st_, so_: (lambda e: e.matmul(
                                             psG[1][:],
                                             w2T[:, hh * 1024 + dd * 512 + jj * 128: hh * 1024 + dd * 512 + (jj + 1) * 128],
                                             QT[:, (2 * hh + dd) * T + bb * 512: (2 * hh + dd) * T + (bb + 1) * 512],
                                             start=st_, stop=so_)))(h, jt, tb, dhc, dhc == 0, dhc == 1),
                                         waits=(w2_ if dhc == 0 else ()), inc=(TP if dhc == 1 else None))
                                C.tp += 1
                                tp_hq = C.tp
                                wsl = [(TP, tp_gq)]
                                if i >= 2:
                                    wsl.append((VP, g_done[i - 2]))
                                C.sp += 1
                                emit(B2, "scalar",
                                     (lambda ii: (lambda e: e.activation(gq_st[ii % 2][:], psG[0][:], AF.Silu)))(i),
                                     waits=wsl, inc=SP)
                                sil_done[i] = C.sp
                                C.vp += 1
                                emit(B2, "vector",
                                     (lambda jj, bb, ii: (lambda e: e.tensor_mul(
                                         G_sb[:, jj * T + bb * 512:jj * T + (bb + 1) * 512],
                                         psG[1][:], gq_st[ii % 2][:])))(jt, tb, i),
                                     waits=[(TP, tp_hq), (SP, sil_done[i])], inc=VP)
                                g_done[i] = C.vp
                        vp_gall = C.vp
                        for tt in range(NT):
                            w = [(VP, vp_gall)]
                            if tt >= 2:
                                w.append((VP, oev_done[tt - 2]))
                            for jc in range(4):
                                emit(B2, "tensor",
                                     (lambda hh, tt_, jj, st_, so_: (lambda e: e.matmul(
                                         psO[tt_ % 2][:, 0:256],
                                         G_sb[:, jj * T + tt_ * 128:jj * T + (tt_ + 1) * 128],
                                         w1T[:, hh * 1024 + jj * 256: hh * 1024 + (jj + 1) * 256],
                                         start=st_, stop=so_)))(h, tt, jc, jc == 0, jc == 3),
                                     waits=(w if jc == 0 else ()), inc=(TP if jc == 3 else None))
                            C.tp += 1
                            tp_o = C.tp
                            C.vp += 1
                            emit(B2, "vector",
                                 (lambda hh, tt_: (lambda e: e.tensor_add(
                                     x_sb[:, tt_ * D + hh * DH: tt_ * D + (hh + 1) * DH],
                                     psO[tt_ % 2][:, 0:256],
                                     x_sb[:, tt_ * D + hh * DH: tt_ * D + (hh + 1) * DH])))(h, tt),
                                 waits=[(TP, tp_o)], inc=VP)
                            oev_done[tt] = C.vp

                    if dbg:
                        emit(B2, "gpsimd", lambda e: e.dma_start(dbgQT_d, QT[:]),
                             waits=[(VP, qt_evac[NT - 1])], inc=SDBG, inc_by=16)
                    b2_tp, b2_sp, b2_vp = C.tp, C.sp, C.vp

            with ExitStack() as s3:
                e3 = s3.enter_context
                xln2_st = [e3(nc.sbuf_tensor(f"x2ln{i}", [128, D], BF16)) for i in range(2)]
                redT = e3(nc.sbuf_tensor("redT", [16, T], BF16))
                wgn_pad = e3(nc.sbuf_tensor("wgn_pad", [128, 16], BF16))
                wgsc = e3(nc.sbuf_tensor("wgsc", [8, 20], F32))
                psT3 = [e3(nc.psum_tensor(f"psT3{i}", [128, 1024], BF16)) for i in range(2)]
                psR = e3(nc.psum_tensor("psR", [16, T], F32))
                psL = [e3(nc.psum_tensor(f"psL{i}", [128, 512], F32)) for i in range(2)]

                B3 = new_block()
                barB = [(TP, b2_tp), (SP, b2_sp), (VP, b2_vp)]

                C.sp += 1
                emit(B3, "scalar", lambda e: e.activation(wgsc[:, 0:16], wg_sb[:], AF.Square),
                     waits=barB + [(SSM, C.s_small)], inc=SP)
                sp_wg0 = C.sp
                C.vp += 1
                emit(B3, "vector", lambda e: e.reduce_sum(wgsc[:, 16:17], wgsc[:, 0:16], axis=AX.X),
                     waits=barB + [(SP, sp_wg0)], inc=VP)
                vp_wg1 = C.vp
                C.sp += 1
                emit(B3, "scalar", lambda e: e.activation(wgsc[:, 17:18], wgsc[:, 16:17], AF.Sqrt),
                     waits=[(VP, vp_wg1)], inc=SP)
                sp_wg1 = C.sp
                C.vp += 1
                emit(B3, "vector", lambda e: e.tensor_scalar(wgsc[:, 18:19], wgsc[:, 17:18], 1.0, 1e-4, ALU.mult, ALU.max),
                     waits=[(SP, sp_wg1)], inc=VP)
                v_wg3 = C.vp
                C.sp += 1
                emit(B3, "scalar", lambda e: e.activation(wgsc[:, 19:20], wgsc[:, 18:19], AF.Identity),
                     waits=[(VP, v_wg3)], inc=SP)
                sp_wg4 = C.sp
                C.vp += 1
                emit(B3, "vector", lambda e: e.reciprocal(wgsc[:, 16:17], wgsc[:, 19:20]),
                     waits=[(SP, sp_wg4)], inc=VP)
                v_wg5 = C.vp
                C.vp += 1
                emit(B3, "vector", lambda e: e.memset(wgn_pad[:], 0.0), waits=barB, inc=VP)
                vp_wgall = C.vp
                C.sp += 1
                emit(B3, "scalar", lambda e: e.activation(wgn_pad[0:8, :], wg_sb[:], AF.Identity, scale=wgsc[:, 16:17]),
                     waits=[(VP, vp_wgall), (VP, v_wg5)], inc=SP)
                sp_wgn = C.sp
                emit(B3, "tensor", lambda e: e.transpose(psT3[0][0:16, 0:128], wgn_pad[:], ident[:]),
                     waits=barB + [(SP, sp_wgn), (SSM, C.s_small)], inc=TP)
                C.tp += 1
                tp_wgt = C.tp
                C.sp += 1
                emit(B3, "scalar", lambda e: e.copy(wgnT[:], psT3[0][0:16, 0:8]), waits=[(TP, tp_wgt)], inc=SP)
                sp_wgnT = C.sp

                x1t_done = [0] * NT
                ssq2_done = [0] * NT
                for t in range(NT):
                    a = t % 2
                    extra = barB if t == 0 else []
                    xws = list(extra)
                    if t >= 1:
                        xws.append((VP, ssq2_done[t - 1]))
                    ln_sp, ssq2_done[t] = ln_stats(B3, x_sb, t * D, xln2_st[a], 1e-5,
                                     w_scalar=xws, w_vector=list(extra))
                    w = [(SP, ln_sp)]
                    if t == 0:
                        w.append((SP, sp_wgnT))
                    if t >= 2:
                        w.append((VP, x1t_done[t - 2]))
                    for k in range(ND):
                        emit(B3, "tensor",
                             (lambda aa, kk: (lambda e: e.transpose(psT3[aa][:, kk * 128:(kk + 1) * 128],
                                                                    xln2_st[aa][:, kk * 128:(kk + 1) * 128], ident[:])))(a, k),
                             waits=(w if k == 0 else ()), inc=(TP if k == ND - 1 else None))
                    C.tp += 1
                    tpv = C.tp
                    C.vp += 1
                    emit(B3, "vector",
                         (lambda aa, tt: (lambda e: e.tensor_copy(
                             X1T[:].rearrange("p (i c) -> p i c", i=ND)[:, :, tt * 128:(tt + 1) * 128],
                             psT3[aa][:].rearrange("p (i c) -> p i c", i=ND))))(a, t),
                         waits=[(TP, tpv)], inc=VP)
                    x1t_done[t] = C.vp
                vp_x1t_all = C.vp

                for k in range(ND):
                    for tb in range(4):
                        last = (k == ND - 1 and tb == 3)
                        emit(B3, "tensor",
                             (lambda kk, bb, st_, so_: (lambda e: e.matmul(
                                 psR[:, bb * 512:(bb + 1) * 512],
                                 wred_bf[:, kk * 16:(kk + 1) * 16],
                                 X1T[:, kk * T + bb * 512: kk * T + (bb + 1) * 512],
                                 start=st_, stop=so_)))(k, tb, k == 0, k == ND - 1),
                             waits=([(VP, vp_x1t_all), (SWQ, 32)] if (k == 0 and tb == 0) else ()),
                             inc=(TP if last else None))
                C.tp += 1
                tp_red = C.tp
                C.sp += 1
                emit(B3, "scalar", lambda e: e.copy(redT[:], psR[:]), waits=[(TP, tp_red)], inc=SP)
                sp_redbf = C.sp

                lg_done = [0] * NT
                d8_done = [0] * NT
                for t in range(NT):
                    a = t % 2
                    w = [(SP, sp_redbf)]
                    if t >= 2:
                        w.append((SP, lg_done[t - 2]))
                    emit(B3, "tensor",
                         (lambda aa, tt: (lambda e: e.matmul(
                             psL[aa][:, 0:8], redT[:, tt * 128:(tt + 1) * 128], wgnT[:],
                             start=True, stop=True)))(a, t),
                         waits=w, inc=TP)
                    C.tp += 1
                    tp_lg = C.tp
                    wlg = [(TP, tp_lg)]
                    if t >= 2:
                        wlg.append((VP, d8_done[t - 2]))
                    C.sp += 1
                    emit(B3, "scalar", (lambda aa: (lambda e: e.copy(lgS[:, aa * 8:aa * 8 + 8], psL[aa][:, 0:8])))(a),
                         waits=wlg, inc=SP)
                    lg_done[t] = C.sp
                    lgsl = lgS[:, a * 8:a * 8 + 8]
                    C.vp += 1
                    emit(B3, "vector", (lambda ls: (lambda e: e.reduce_max(col(S_NMX), ls, axis=AX.X, negate=True)))(lgsl),
                         waits=[(SP, lg_done[t])], inc=VP)
                    v_nmx = C.vp
                    d8_done[t] = C.vp
                    C.sp += 1
                    emit(B3, "scalar", (lambda ls: (lambda e: e.activation(routS[:, 0:8], ls, AF.Identity, bias=col(S_NMX))))(lgsl),
                         waits=[(VP, v_nmx), (SP, lg_done[t])], inc=SP)
                    sp_d8 = C.sp
                    C.sp += 1
                    emit(B3, "scalar", lambda e: e.sign(routS[:, 8:16], routS[:, 0:8]),
                         waits=[(SP, sp_d8)], inc=SP)
                    sp_sg = C.sp
                    C.sp += 1
                    emit(B3, "scalar", (lambda ls: (lambda e: e.activation(routS[:, 16:24], ls, AF.Exp, bias=col(S_NMX))))(lgsl),
                         inc=SP)
                    sp_sx = C.sp
                    C.vp += 1
                    emit(B3, "vector", lambda e: e.reduce_sum(col(S_ESUM), routS[:, 16:24], axis=AX.X),
                         waits=[(SP, sp_sx)], inc=VP)
                    v_es = C.vp
                    C.sp += 1
                    emit(B3, "scalar", lambda e: e.activation(col(S_REC), col(S_ESUM), AF.Identity),
                         waits=[(VP, v_es)], inc=SP)
                    sp_esc = C.sp
                    C.vp += 1
                    emit(B3, "vector", lambda e: e.reciprocal(col(S_REC2), col(S_REC)),
                         waits=[(SP, sp_esc)], inc=VP)
                    v_rec = C.vp
                    C.sp += 1
                    emit(B3, "scalar", (lambda tt: (lambda e: e.activation(
                        c_sb[:, tt * E:(tt + 1) * E], routS[:, 8:16], AF.Identity,
                        scale=col(S_REC2), bias=col(S_REC2))))(t),
                        waits=[(VP, v_rec), (SP, sp_sg)], inc=SP)
                b3_c_sp = C.sp
                b3_c_vp = C.vp

                for g in range(4):
                    dst = out_d[g * 512:(g + 1) * 512, :].rearrange("(n p) d -> p n d", p=128)
                    src = x_sb[:, g * 4096:(g + 1) * 4096].rearrange("p (n d) -> p n d", n=4)
                    C.s_base += 16
                    emit(B3, "sync", (lambda d_, s_: (lambda e: e.dma_start(d_, s_)))(dst, src),
                         waits=(barB if g == 0 else ()), inc=SB, inc_by=16)

                if dbg:
                    emit(B3, "gpsimd", lambda e: e.dma_start(dbgX1T_d, X1T[:]),
                         waits=[(VP, vp_x1t_all)], inc=SDBG, inc_by=16)
                    emit(B3, "gpsimd", lambda e: e.dma_start(dbgRED_d, redT[:]),
                         waits=[(SP, sp_redbf)], inc=SDBG, inc_by=16)
                    emit(B3, "gpsimd", lambda e: e.dma_start(dbgC_d, c_sb[:]),
                         waits=[(SP, b3_c_sp)], inc=SDBG, inc_by=16)
                if dbg & 2:
                    emit(B3, "gpsimd", lambda e: e.nop(),
                         waits=[(SB, C.s_base), (SDBG, 64)])
                b3_tp, b3_sp, b3_vp = C.tp, C.sp, C.vp

        if dbg & 2:
            check_deadlock()
            for bb in blocks:
                with nc.Block() as blk:
                    for eng_name in ENGS:
                        ops = bb[eng_name]
                        if not ops:
                            continue

                        def runner(ops_):
                            def _(eng):
                                for f in ops_:
                                    f(eng)
                            return _
                        getattr(blk, eng_name)(runner(ops))
            return nc
        with ExitStack() as s4:
            e4 = s4.enter_context
            HT = e4(nc.sbuf_tensor("HT", [128, NFC * T], BF16))
            w1sl = [e4(nc.sbuf_tensor(f"w1sl{i}", [128, 4096], BF16)) for i in range(2)]
            w2buf = [e4(nc.sbuf_tensor(f"w2buf{i}", [128, NFC * D], BF16)) for i in range(2)]
            ytmp = [e4(nc.sbuf_tensor(f"ytmp{i}", [128, D], F32)) for i in range(2)]
            b2bf = e4(nc.sbuf_tensor("b2bf", [1, E * D], BF16))
            psH = [e4(nc.psum_tensor(f"psH{i}", [128, 512], F32)) for i in range(2)]
            psY = [e4(nc.psum_tensor(f"psY{i}", [128, 1024], F32)) for i in range(2)]

            B4 = new_block()
            barC = [(TP, b3_tp), (SP, b3_sp), (VP, b3_vp), (SB, C.s_base)]

            units = [(e_, hf_) for e_ in range(E) for hf_ in range(2)]

            w1_slot_uses = {0: [], 1: []}
            w1_slab_ready = {}
            w2_ready = {}
            w1_consumed_tp = {}
            w2_consumed_tp = {}
            slab_idx = 0
            for ui, (e_, hf_) in enumerate(units):
                for s_ in range(4):
                    slot = slab_idx % 2
                    w1_slot_uses[slot].append((e_, hf_, s_))
                    C.s_w1[slot] += 32
                    w1_slab_ready[(e_, hf_, s_)] = (slot, C.s_w1[slot])
                    slab_idx += 1
                C.s_w2[ui % 2] += 64
                w2_ready[(e_, hf_)] = C.s_w2[ui % 2]

            gelu_hist = []
            gelu_done = {}
            ytmp_done = {}
            acc_k = [0, 0]
            accums = []

            for ui, (e_, hf_) in enumerate(units):
                for s_ in range(4):
                    slot, w1cnt = w1_slab_ready[(e_, hf_, s_)]
                    for ft in range(4):
                        fcl = s_ * 4 + ft
                        for tb in range(4):
                            gi = len(gelu_hist)
                            aa = gi % 2
                            w = []
                            if ft == 0 and tb == 0:
                                w.append((SW1[slot], w1cnt))
                                if ui == 0 and s_ == 0:
                                    w += barC
                            if gi >= 2:
                                w.append((SP, gelu_hist[gi - 2]))
                            for k in range(ND):
                                emit(B4, "tensor",
                                     (lambda aa_, slt, kk, ff, bb, st_, so_: (lambda e: e.matmul(
                                         psH[aa_][:],
                                         w1sl[slt][:, kk * 512 + ff * 128: kk * 512 + (ff + 1) * 128],
                                         X1T[:, kk * T + bb * 512: kk * T + (bb + 1) * 512],
                                         start=st_, stop=so_)))(aa, slot, k, ft, tb, k == 0, k == ND - 1),
                                     waits=(w if k == 0 else ()), inc=(TP if k == ND - 1 else None))
                            C.tp += 1
                            tp_g = C.tp
                            wg_ = [(TP, tp_g)]
                            if ui == 0 and fcl == 0 and tb == 0:
                                wg_.append((SB, C.s_base))
                            C.sp += 1
                            emit(B4, "scalar",
                                 (lambda aa_, ee, hh, ff, bb: (lambda e: e.activation(
                                     HT[:, ff * T + bb * 512: ff * T + (bb + 1) * 512],
                                     psH[aa_][:], AF.Gelu,
                                     bias=b1c[:, ee * 32 + hh * 16 + ff: ee * 32 + hh * 16 + ff + 1])))(aa, e_, hf_, fcl, tb),
                                 waits=wg_, inc=SP)
                            gelu_hist.append(C.sp)
                    w1_consumed_tp[(e_, hf_, s_)] = C.tp
                gelu_done[(e_, hf_)] = C.sp

                for tc in range(NT):
                    b = tc % 2
                    w = [(SP, gelu_done[(e_, hf_)]), (SW2[ui % 2], w2_ready[(e_, hf_)])]
                    key2 = (e_, hf_, tc - 2)
                    if key2 in ytmp_done:
                        w.append((SP, ytmp_done[key2]))
                    elif ui > 0:
                        pe, ph = units[ui - 1]
                        w.append((SP, ytmp_done[(pe, ph, NT - 2 + tc)]))
                    if hf_ == 0:
                        for dh in range(2):
                            emit(B4, "tensor",
                                 (lambda bb, ee, dd: (lambda e: e.matmul(
                                     psY[bb][:, dd * 512:(dd + 1) * 512],
                                     ones1[0:1, :], b2bf[0:1, ee * D + dd * 512: ee * D + (dd + 1) * 512],
                                     start=True, stop=False)))(b, e_, dh),
                                 waits=((w + [(SB2, 16)]) if dh == 0 else ()))
                    for fcl in range(NFC):
                        for dh in range(2):
                            last_all = (fcl == NFC - 1 and dh == 1)
                            emit(B4, "tensor",
                                 (lambda bb, ff, tt, dd, st__, so__, wb: (lambda e: e.matmul(
                                     psY[bb][:, dd * 512:(dd + 1) * 512],
                                     HT[:, ff * T + tt * 128: ff * T + (tt + 1) * 128],
                                     w2buf[wb][:, ff * D + dd * 512: ff * D + (dd + 1) * 512],
                                     start=st__, stop=so__)))(b, fcl, tc, dh,
                                                              (hf_ == 1 and fcl == 0), fcl == NFC - 1, ui % 2),
                                 waits=(w if (hf_ == 1 and fcl == 0 and dh == 0) else ()),
                                 inc=(TP if last_all else None))
                    C.tp += 1
                    tp_y = C.tp
                    par = tc % 2
                    kclass = acc_k[par]
                    wy = [(TP, tp_y), (SP, b3_c_sp)]
                    if kclass >= 1:
                        wy.append((SAC[par], 16 * kclass))
                    C.sp += 1
                    emit(B4, "scalar",
                         (lambda bb, ee, tt: (lambda e: e.activation(
                             ytmp[tt % 2][:], psY[bb][:], AF.Identity,
                             scale=c_sb[:, tt * E + ee: tt * E + ee + 1])))(b, e_, tc),
                         waits=wy, inc=SP)
                    ytmp_done[(e_, hf_, tc)] = C.sp
                    acc_k[par] += 1
                    accums.append((e_, hf_, tc, par, acc_k[par], C.sp))
                w2_consumed_tp[(e_, hf_)] = C.tp

            def emit_w1(e_, hf_, s_, slot, first_waits=()):
                uses = w1_slot_uses[slot]
                myi = uses.index((e_, hf_, s_))
                w = list(first_waits)
                if myi >= 1:
                    w.append((TP, w1_consumed_tp[uses[myi - 1]]))
                fc0 = hf_ * FH + s_ * 512
                for g in range(2):
                    src = w1_d[e_, g * 512:(g + 1) * 512, fc0:fc0 + 512].rearrange("(c p) f -> p c f", p=128)
                    dst = w1sl[slot][:, g * 2048:(g + 1) * 2048].rearrange("p (c f) -> p c f", c=4)
                    emit(B4, "gpsimd", (lambda d_, s__: (lambda e: e.dma_start(d_, s__)))(dst, src),
                         waits=(w if g == 0 else ()), inc=SW1[slot], inc_by=16)

            def emit_w2(e_, hf_, r, first_waits=()):
                ui_ = e_ * 2 + hf_
                w = list(first_waits)
                if ui_ >= 2:
                    w.append((TP, w2_consumed_tp[units[ui_ - 2]]))
                for g in range(4):
                    src = w2_d[e_, hf_ * FH + g * 512: hf_ * FH + (g + 1) * 512, :].rearrange("(c p) d -> p c d", p=128)
                    dst = w2buf[r][:, g * 4096:(g + 1) * 4096].rearrange("p (c d) -> p c d", c=4)
                    emit(B4, "gpsimd", (lambda d_, s__: (lambda e: e.dma_start(d_, s__)))(dst, src),
                         waits=(w if g == 0 else ()), inc=SW2[r], inc_by=16)

            emit(B4, "gpsimd", lambda e: e.dma_start(b2bf[:], b2r_d), waits=barC, inc=SB2, inc_by=16)
            emit_w1(0, 0, 0, w1_slab_ready[(0, 0, 0)][0])
            emit_w1(0, 0, 1, w1_slab_ready[(0, 0, 1)][0])
            emit_w2(0, 0, 0)
            emit_w1(0, 0, 2, w1_slab_ready[(0, 0, 2)][0])
            emit_w1(0, 0, 3, w1_slab_ready[(0, 0, 3)][0])

            acc_by_unit = {}
            for a_ in accums:
                acc_by_unit.setdefault((a_[0], a_[1]), []).append(a_)

            def emit_acc(a_):
                ea, ha, tca, para, kcla, spva = a_
                w = [(SP, spva)]
                if ea == 0 and ha == 0 and tca == 0:
                    w.append((SB, C.s_base))
                emit(B4, "gpsimd",
                     (lambda tt: (lambda e: e.dma_start(
                         out_d[tt * 128:(tt + 1) * 128, :], ytmp[tt % 2][:], accum_op=ALU.add)))(tca),
                     waits=w, inc=SAC[para], inc_by=16)

            for ui, (e_, hf_) in enumerate(units):
                aops = acc_by_unit[(e_, hf_)]
                early = []
                late = []
                if ui + 1 < len(units):
                    ne, nh = units[ui + 1]
                    early = [("w1", (ne, nh, 0, w1_slab_ready[(ne, nh, 0)][0])),
                             ("w1", (ne, nh, 1, w1_slab_ready[(ne, nh, 1)][0])),
                             ("w2", (ne, nh, (ui + 1) % 2))]
                    late = [("w1", (ne, nh, 2, w1_slab_ready[(ne, nh, 2)][0])),
                            ("w1", (ne, nh, 3, w1_slab_ready[(ne, nh, 3)][0]))]
                for i, a_ in enumerate(aops):
                    if i < len(early):
                        kind, pl = early[i]
                        (emit_w1 if kind == "w1" else emit_w2)(*pl)
                    emit_acc(a_)
                for kind, pl in late:
                    (emit_w1 if kind == "w1" else emit_w2)(*pl)

            emit(B4, "gpsimd", lambda e: e.nop(),
                 waits=[(SAC[0], 16 * acc_k[0]), (SAC[1], 16 * acc_k[1])] +
                       ([(SDBG, 64)] if dbg else []))

            check_deadlock()

            for bb in blocks:
                with nc.Block() as blk:
                    for eng_name in ENGS:
                        ops = bb[eng_name]
                        if not ops:
                            continue

                        def runner(ops_):
                            def _(eng):
                                for f in ops_:
                                    f(eng)
                            return _
                        getattr(blk, eng_name)(runner(ops))
    return nc


def kernel(**inputs):
    x = np.asarray(inputs["L0_attn_hidden_states"], dtype=np.float32)
    qkv_w = np.asarray(inputs["L0_attn_qkv_w"], dtype=np.float32)
    q_norm_w = np.asarray(inputs["L0_attn_q_norm_w"], dtype=np.float32)
    w0 = np.asarray(inputs["L0_attn_w0"], dtype=np.float32)
    w1a = np.asarray(inputs["L0_attn_w1"], dtype=np.float32)
    w2a = np.asarray(inputs["L0_attn_w2"], dtype=np.float32)
    w_red = np.asarray(inputs["L0_ffn_w_red"], dtype=np.float32)
    wg = np.asarray(inputs["L0_ffn_wg"], dtype=np.float32)
    W1 = np.asarray(inputs["L0_ffn_w1"], dtype=np.float32)
    b1 = np.asarray(inputs["L0_ffn_b1"], dtype=np.float32)
    W2 = np.asarray(inputs["L0_ffn_w2"], dtype=np.float32)
    b2 = np.asarray(inputs["L0_ffn_b2"], dtype=np.float32)

    Wq = qkv_w[:D]
    wqt = np.ascontiguousarray(Wq.T.reshape(ND, 128, D).transpose(1, 0, 2).reshape(128, ND * D))
    w0t = np.ascontiguousarray(np.stack([w0[h].T.reshape(2, 128, J) for h in range(NH)])
                               .transpose(2, 0, 1, 3).reshape(128, 4096))
    w2t = np.ascontiguousarray(np.stack([w2a[h].T.reshape(2, 128, J) for h in range(NH)])
                               .transpose(2, 0, 1, 3).reshape(128, 4096))
    w1t = np.ascontiguousarray(np.stack([w1a[h].T.reshape(4, 128, DH) for h in range(NH)])
                               .transpose(2, 0, 1, 3).reshape(128, 4096))
    wredl = np.ascontiguousarray(w_red.reshape(ND, 128, 16).transpose(1, 0, 2).reshape(128, 128))
    qnw_rep = np.ascontiguousarray(np.broadcast_to(q_norm_w, (128, D)))
    b1c = np.ascontiguousarray(b1.reshape(E, 32, 128).transpose(2, 0, 1).reshape(128, 256))
    b2r = np.ascontiguousarray(b2.reshape(1, E * D))
    ident = np.eye(128, dtype=ml_dtypes.bfloat16)
    ones1 = np.ones((1, 128), dtype=ml_dtypes.bfloat16)

    nc = build_nc(dbg=_DBG[0])
    common = dict(wqt=wqt, w0t=w0t, w2t=w2t, w1t=w1t, wred=wredl, wg=wg, qnw=qnw_rep,
                  b1c=b1c, b2r=b2r, ident=ident, ones1=ones1, w1e=W1, w2e=W2)
    in_maps = [dict(common, x=np.ascontiguousarray(x[i])) for i in range(8)]
    res = run_bass_kernel_spmd(nc, in_maps, core_ids=list(range(8)))
    if _DBG[0]:
        _DBG[1] = res.results
    return np.stack([r["out"] for r in res.results]).astype(np.float32)


_DBG = [0, None]


# version 07
# speedup vs baseline: 1.0059x; 1.0059x over previous
import numpy as np
import ml_dtypes
from contextlib import ExitStack

import concourse.bass as bass
import concourse.mybir as mybir
from concourse.bass_utils import run_bass_kernel_spmd

F32 = mybir.dt.float32
BF16 = mybir.dt.bfloat16
AF = mybir.ActivationFunctionType
ALU = mybir.AluOpType
AX = mybir.AxisListType

T = 2048
NT = 16
D = 1024
ND = 8
NH = 4
DH = 256
J = 512
F = 4096
E = 8
FH = 2048
NFC = 16

ENGS = ("sync", "tensor", "scalar", "vector", "gpsimd")


def build_nc(dbg=0):
    nc = bass.Bass("TRN2", target_bir_lowering=False)

    x_d = nc.dram_tensor("x", [T, D], F32, kind="ExternalInput").ap()
    wqt_d = nc.dram_tensor("wqt", [128, ND * D], F32, kind="ExternalInput").ap()
    w0t_d = nc.dram_tensor("w0t", [128, 4096], F32, kind="ExternalInput").ap()
    w2t_d = nc.dram_tensor("w2t", [128, 4096], F32, kind="ExternalInput").ap()
    w1t_d = nc.dram_tensor("w1t", [128, 4096], F32, kind="ExternalInput").ap()
    wred_d = nc.dram_tensor("wred", [128, 128], F32, kind="ExternalInput").ap()
    wg_d = nc.dram_tensor("wg", [8, 16], F32, kind="ExternalInput").ap()
    qnw_d = nc.dram_tensor("qnw", [128, D], F32, kind="ExternalInput").ap()
    b1c_d = nc.dram_tensor("b1c", [128, 256], F32, kind="ExternalInput").ap()
    b2r_d = nc.dram_tensor("b2r", [1, E * D], F32, kind="ExternalInput").ap()
    ident_d = nc.dram_tensor("ident", [128, 128], BF16, kind="ExternalInput").ap()
    ones_d = nc.dram_tensor("ones1", [1, 128], BF16, kind="ExternalInput").ap()
    w1_d = nc.dram_tensor("w1e", [E, D, F], F32, kind="ExternalInput").ap()
    w2_d = nc.dram_tensor("w2e", [E, F, D], F32, kind="ExternalInput").ap()
    out_d = nc.dram_tensor("out", [T, D], F32, kind="ExternalOutput").ap()
    if dbg:
        dbgQT_d = nc.dram_tensor("dbgQT", [128, ND * T], BF16, kind="ExternalOutput").ap()
        dbgX1T_d = nc.dram_tensor("dbgX1T", [128, ND * T], BF16, kind="ExternalOutput").ap()
        dbgC_d = nc.dram_tensor("dbgC", [128, NT * E], F32, kind="ExternalOutput").ap()
        dbgRED_d = nc.dram_tensor("dbgRED", [16, T], BF16, kind="ExternalOutput").ap()

    blocks = []

    def new_block():
        b = {e: [] for e in ENGS}
        blocks.append(b)
        return b

    OPS = []
    sem_hist = {}
    sem_cum = {}

    def emit(blk, eng, make, waits=(), inc=None, inc_by=1):
        wl = [(h, v) for (h, v) in waits]

        def fn(e):
            for (h, v) in wl:
                e.wait_ge(h, v)
            ins = make(e)
            if inc is not None:
                ins.then_inc(inc, inc_by)
        blk[eng].append(fn)
        oi = len(OPS)
        rec = dict(eng=eng, waits=[(id(h), v) for (h, v) in wl], inc=None, oi=oi)
        if inc is not None:
            k = id(inc)
            sem_cum[k] = sem_cum.get(k, 0) + inc_by
            sem_hist.setdefault(k, []).append((sem_cum[k], oi))
            rec["inc"] = (k, sem_cum[k])
        OPS.append(rec)

    def check_deadlock():
        n = len(OPS)
        adj = [[] for _ in range(n)]
        indeg = [0] * n
        last_on = {}
        for op in OPS:
            oi = op["oi"]
            if op["eng"] in last_on:
                adj[last_on[op["eng"]]].append(oi)
                indeg[oi] += 1
            last_on[op["eng"]] = oi
            for (k, v) in op["waits"]:
                hist = sem_hist.get(k, [])
                ach = None
                for (cum, j) in hist:
                    if cum >= v:
                        ach = j
                        break
                if ach is None:
                    raise AssertionError(f"op {oi} ({op['eng']}) waits sem value {v} never reached")
                adj[ach].append(oi)
                indeg[oi] += 1
        from collections import deque
        dq = deque(i for i in range(n) if indeg[i] == 0)
        seen = 0
        while dq:
            u = dq.popleft()
            seen += 1
            for v2 in adj[u]:
                indeg[v2] -= 1
                if indeg[v2] == 0:
                    dq.append(v2)
        if seen != n:
            stuck = [i for i in range(n) if indeg[i] > 0][:12]
            detail = [(i, OPS[i]["eng"], OPS[i]["waits"]) for i in stuck]
            raise AssertionError(f"deadlock cycle among ops (first 12): {detail}")

    class C:
        tp = 0; sp = 0; vp = 0
        s_x = 0; s_small = 0; s_base = 0; s_wq = 0; s_wh = 0; s_b2 = 0
        s_w1 = [0, 0]; s_w2 = [0, 0]; s_acc = [0, 0]

    with ExitStack() as st:
        en = st.enter_context
        TP = en(nc.semaphore("TP")); SP = en(nc.semaphore("SP")); VP = en(nc.semaphore("VP"))
        SX = en(nc.semaphore("SX")); SSM = en(nc.semaphore("SSM")); SB = en(nc.semaphore("SBASE"))
        SWQ = en(nc.semaphore("SWQ")); SWH = en(nc.semaphore("SWH")); SB2 = en(nc.semaphore("SB2"))
        SW1 = [en(nc.semaphore("SW1a")), en(nc.semaphore("SW1b"))]
        SW2 = [en(nc.semaphore("SW2a")), en(nc.semaphore("SW2b"))]
        SAC = [en(nc.semaphore("SACa")), en(nc.semaphore("SACb"))]
        SDBG = en(nc.semaphore("SDBG"))

        X1T = en(nc.sbuf_tensor("X1T", [128, ND * T], BF16))
        c_sb = en(nc.sbuf_tensor("c_sb", [128, NT * E], F32))
        ident = en(nc.sbuf_tensor("identb", [128, 128], BF16))
        ones1 = en(nc.sbuf_tensor("ones1b", [1, 128], BF16))
        b1c = en(nc.sbuf_tensor("b1cb", [128, 256], F32))
        wg_sb = en(nc.sbuf_tensor("wg_sb", [8, 16], F32))
        wred_bf = en(nc.sbuf_tensor("wred_bf", [128, 128], BF16))
        wgnT = en(nc.sbuf_tensor("wgnT", [16, 8], BF16))
        cols = en(nc.sbuf_tensor("cols", [128, 24], F32))
        lgS = en(nc.sbuf_tensor("lgS", [128, 16], F32))
        routS = en(nc.sbuf_tensor("routS", [128, 24], F32))

        (S_SUM, S_SSQ, S_M, S_E2, S_MSQ, S_VAR, S_SD, S_RS, S_NB,
         S_SS2, S_SD2, S_RS2, S_NMX, S_ESUM, S_REC, S_REC2, S_SD2B) = range(17)

        def col(i):
            return cols[:, i:i + 1]

        def ln_stats(B, src_sb, off, dst_stage, eps, w_scalar=(), w_vector=()):
            xsl = src_sb[:, off:off + D]
            C.sp += 1
            emit(B, "scalar", (lambda xs: (lambda e: e.activation(sqscr[:], xs, AF.Square)))(xsl),
                 waits=list(w_scalar), inc=SP)
            sp_sq = C.sp
            C.vp += 1
            emit(B, "vector", (lambda xs: (lambda e: e.reduce_sum(col(S_SUM), xs, axis=AX.X)))(xsl),
                 waits=list(w_vector), inc=VP)
            v_sum = C.vp
            C.vp += 1
            emit(B, "vector", lambda e: e.reduce_sum(col(S_SSQ), sqscr[:], axis=AX.X),
                 waits=[(SP, sp_sq)], inc=VP)
            v_ssq = C.vp
            C.vp += 1
            emit(B, "vector", lambda e: e.tensor_scalar(col(S_M), col(S_SUM), 1.0 / D, 0.0, ALU.mult, ALU.add),
                 waits=[(VP, v_sum)], inc=VP)
            v_m = C.vp
            C.vp += 1
            emit(B, "vector", (lambda ep: (lambda e: e.tensor_scalar(col(S_E2), col(S_SSQ), 1.0 / D, ep, ALU.mult, ALU.add)))(eps),
                 waits=[(VP, v_ssq)], inc=VP)
            C.vp += 1
            emit(B, "vector", lambda e: e.tensor_mul(col(S_MSQ), col(S_M), col(S_M)),
                 waits=[(VP, v_m)], inc=VP)
            v_msq = C.vp
            C.vp += 1
            emit(B, "vector", lambda e: e.tensor_scalar(col(S_VAR), col(S_MSQ), -1.0, col(S_E2), ALU.mult, ALU.add),
                 waits=[(VP, v_msq)], inc=VP)
            vp_var = C.vp
            C.sp += 1
            emit(B, "scalar", lambda e: e.activation(col(S_SD), col(S_VAR), AF.Sqrt),
                 waits=[(VP, vp_var)], inc=SP)
            sp_sd = C.sp
            C.vp += 1
            emit(B, "vector", lambda e: e.reciprocal(col(S_RS), col(S_SD)), waits=[(SP, sp_sd)], inc=VP)
            v_rs = C.vp
            C.vp += 1
            emit(B, "vector", lambda e: e.tensor_scalar(col(S_NB), col(S_M), col(S_RS), -1.0, ALU.mult, ALU.mult),
                 waits=[(VP, v_rs)], inc=VP)
            vp_nb = C.vp
            C.sp += 1
            emit(B, "scalar", (lambda xs, dst: (lambda e: e.activation(dst[:], xs, AF.Identity, bias=col(S_NB), scale=col(S_RS))))(xsl, dst_stage),
                 waits=[(VP, vp_nb)], inc=SP)
            return C.sp, v_ssq

        with ExitStack() as sxs:
            ex_ = sxs.enter_context
            x_sb = ex_(nc.sbuf_tensor("x_sb", [128, NT * D], F32))
            sqscr = ex_(nc.sbuf_tensor("sqscr", [128, D], F32))

            with ExitStack() as sqt:
                eq_ = sqt.enter_context
                QT = eq_(nc.sbuf_tensor("QT", [128, ND * T], BF16))

                with ExitStack() as s1:
                    e1 = s1.enter_context
                    WqT = e1(nc.sbuf_tensor("WqT", [128, ND * D], BF16))
                    qnw = e1(nc.sbuf_tensor("qnw_sb", [128, D], F32))
                    xln_st = [e1(nc.sbuf_tensor(f"xln{i}", [128, D], BF16)) for i in range(2)]
                    xt0_st = [e1(nc.sbuf_tensor(f"xt0{i}", [128, D], BF16)) for i in range(2)]
                    qn_st = [e1(nc.sbuf_tensor(f"qn{i}", [128, D], BF16)) for i in range(2)]
                    qw_sb = e1(nc.sbuf_tensor("qw", [128, D], F32))
                    qw2 = e1(nc.sbuf_tensor("qw2", [128, D], F32))
                    h4 = e1(nc.sbuf_tensor("h4", [128, 16], F32))
                    psT = [e1(nc.psum_tensor("psT0", [128, 1024], BF16))]
                    psT2 = [e1(nc.psum_tensor("psT20", [128, 1024], BF16))]
                    psQ = [e1(nc.psum_tensor(f"psQ{i}", [128, 1024], F32)) for i in range(3)]

                    B1 = new_block()

                    for g in range(4):
                        src = x_d[g * 512:(g + 1) * 512, :].rearrange("(n p) d -> p n d", p=128)
                        dst = x_sb[:, g * 4096:(g + 1) * 4096].rearrange("p (n d) -> p n d", n=4)
                        C.s_x += 16
                        emit(B1, "sync", (lambda d_, s_: (lambda e: e.dma_start(d_, s_)))(dst, src),
                             inc=SX, inc_by=16)
                    for (dst, src) in ((qnw[:], qnw_d), (b1c[:], b1c_d), (wg_sb[:], wg_d),
                                       (ident[:], ident_d), (ones1[:], ones_d)):
                        C.s_small += 16
                        emit(B1, "sync", (lambda d_, s_: (lambda e: e.dma_start(d_, s_)))(dst, src),
                             inc=SSM, inc_by=16)
                    C.s_wq += 16
                    emit(B1, "gpsimd", lambda e: e.dma_start(WqT[:], wqt_d), inc=SWQ, inc_by=16)
                    C.s_wq += 16
                    emit(B1, "gpsimd", lambda e: e.dma_start(wred_bf[:], wred_d), inc=SWQ, inc_by=16)

                    xs_done = [0] * NT; q_done = [0] * NT; tq_done = [0] * NT
                    qt_evac = [0] * NT; qn_done = [0] * NT; qcons_v = [0] * NT
                    hs_done = [0] * NT; ssq_done = [0] * NT

                    for t in range(NT):
                        a = t % 2
                        aq = t % 3
                        xw = [(SX, 64)]
                        xws = list(xw)
                        if t >= 1:
                            xws.append((VP, ssq_done[t - 1]))
                        ln_sp, ssq_done[t] = ln_stats(B1, x_sb, t * D, xln_st[a], 1e-5, w_scalar=xws, w_vector=xw)
                        w = [(SP, ln_sp), (SSM, C.s_small)]
                        if t >= 1:
                            w.append((SP, xs_done[t - 1]))
                        for k in range(ND):
                            emit(B1, "tensor",
                                 (lambda kk: (lambda e: e.transpose(psT[0][:, kk * 128:(kk + 1) * 128],
                                                                    xln_st[t % 2][:, kk * 128:(kk + 1) * 128], ident[:])))(k),
                                 waits=(w if k == 0 else ()), inc=(TP if k == ND - 1 else None))
                        C.tp += 1
                        tx_tp = C.tp
                        C.sp += 1
                        emit(B1, "scalar", (lambda aa: (lambda e: e.copy(xt0_st[aa][:], psT[0][:])))(a),
                             waits=[(TP, tx_tp)], inc=SP)
                        xs_done[t] = C.sp
                        w = [(SP, xs_done[t]), (SWQ, 32)]
                        if t >= 3:
                            w.append((VP, qcons_v[t - 3]))
                        for k in range(ND):
                            for hlf in range(2):
                                last = (k == ND - 1 and hlf == 1)
                                emit(B1, "tensor",
                                     (lambda qq, aa, kk, hh, st_, so_: (lambda e: e.matmul(
                                         psQ[qq][:, hh * 512:(hh + 1) * 512],
                                         xt0_st[aa][:, kk * 128:(kk + 1) * 128],
                                         WqT[:, kk * D + hh * 512: kk * D + (hh + 1) * 512],
                                         start=st_, stop=so_)))(aq, a, k, hlf, k == 0, k == ND - 1),
                                     waits=(w if (k == 0 and hlf == 0) else ()),
                                     inc=(TP if last else None))
                        C.tp += 1
                        q_done[t] = C.tp
                        wq2 = [(TP, q_done[t])]
                        if t >= 1:
                            wq2.append((VP, hs_done[t - 1]))
                        C.sp += 1
                        emit(B1, "scalar", (lambda qq: (lambda e: e.activation(qw2[:], psQ[qq][:], AF.Square)))(aq),
                             waits=wq2, inc=SP)
                        sp_q2 = C.sp
                        C.vp += 1
                        emit(B1, "vector", lambda e: e.reduce_sum(col(S_SS2), qw2[:], axis=AX.X),
                             waits=[(SP, sp_q2)], inc=VP)
                        v_ss2 = C.vp
                        C.vp += 1
                        emit(B1, "vector", lambda e: e.tensor_scalar(col(S_SD2), col(S_SS2), 1.0 / D, 1e-6, ALU.mult, ALU.add),
                             waits=[(VP, v_ss2)], inc=VP)
                        vp_sd2p = C.vp
                        C.sp += 1
                        emit(B1, "scalar", lambda e: e.activation(col(S_SD2B), col(S_SD2), AF.Sqrt),
                             waits=[(VP, vp_sd2p)], inc=SP)
                        sp_sd2 = C.sp
                        C.vp += 1
                        emit(B1, "vector", (lambda qq: (lambda e: e.tensor_mul(qw_sb[:], psQ[qq][:], qnw[:])))(aq),
                             waits=[(TP, q_done[t]), (SSM, C.s_small)], inc=VP)
                        v_qw = C.vp
                        C.vp += 1
                        emit(B1, "vector", lambda e: e.reciprocal(col(S_RS2), col(S_SD2B)), waits=[(SP, sp_sd2)], inc=VP)
                        v_rs2 = C.vp
                        C.vp += 1
                        emit(B1, "vector", lambda e: e.tensor_mul(qw2[:], qw_sb[:], qw_sb[:]),
                             waits=[(VP, v_qw), (SP, sp_q2)], inc=VP)
                        v_qw2 = C.vp
                        for h in range(NH):
                            C.vp += 1
                            emit(B1, "vector",
                                 (lambda hh: (lambda e: e.reduce_sum(h4[:, hh:hh + 1], qw2[:, hh * DH:(hh + 1) * DH], axis=AX.X)))(h),
                                 waits=([(VP, v_qw2)] if h == 0 else ()), inc=VP)
                        vp_hs = C.vp
                        hs_done[t] = C.vp
                        C.sp += 1
                        emit(B1, "scalar", lambda e: e.activation(h4[:, 4:8], h4[:, 0:4], AF.Sqrt),
                             waits=[(VP, vp_hs)], inc=SP)
                        sp_hsr = C.sp
                        C.vp += 1
                        emit(B1, "vector", lambda e: e.tensor_scalar(h4[:, 8:12], h4[:, 4:8], col(S_RS2), 1e-5, ALU.mult, ALU.add),
                             waits=[(SP, sp_hsr), (VP, v_rs2)], inc=VP)
                        v_t4 = C.vp
                        C.vp += 1
                        emit(B1, "vector", lambda e: e.reciprocal(h4[:, 12:16], h4[:, 8:12]),
                             waits=[(VP, v_t4)], inc=VP)
                        v_t5 = C.vp
                        C.vp += 1
                        emit(B1, "vector", lambda e: e.tensor_scalar(h4[:, 12:16], h4[:, 12:16], col(S_RS2), 0.0, ALU.mult, ALU.add),
                             waits=[(VP, v_t5)], inc=VP)
                        qcons_v[t] = C.vp
                        vp_fs = C.vp
                        for h in range(NH):
                            C.sp += 1
                            emit(B1, "scalar",
                                 (lambda aa, hh: (lambda e: e.activation(
                                     qn_st[aa][:, hh * DH:(hh + 1) * DH], qw_sb[:, hh * DH:(hh + 1) * DH],
                                     AF.Identity, scale=h4[:, 12 + hh:13 + hh])))(a, h),
                                 waits=([(VP, vp_fs)] if h == 0 else ()), inc=SP)
                        qn_done[t] = C.sp
                        w = [(SP, qn_done[t])]
                        if t >= 1:
                            w.append((VP, qt_evac[t - 1]))
                        for k in range(ND):
                            emit(B1, "tensor",
                                 (lambda aa, kk: (lambda e: e.transpose(psT2[0][:, kk * 128:(kk + 1) * 128],
                                                                        qn_st[aa][:, kk * 128:(kk + 1) * 128], ident[:])))(a, k),
                                 waits=(w if k == 0 else ()), inc=(TP if k == ND - 1 else None))
                        C.tp += 1
                        tq_done[t] = C.tp
                        C.vp += 1
                        emit(B1, "vector",
                             (lambda tt: (lambda e: e.tensor_copy(
                                 QT[:].rearrange("p (i c) -> p i c", i=ND)[:, :, tt * 128:(tt + 1) * 128],
                                 psT2[0][:].rearrange("p (i c) -> p i c", i=ND))))(t),
                             waits=[(TP, tq_done[t])], inc=VP)
                        qt_evac[t] = C.vp

                    b1_tp, b1_sp, b1_vp = C.tp, C.sp, C.vp

                with ExitStack() as s2:
                    e2 = s2.enter_context
                    w0T = e2(nc.sbuf_tensor("w0T", [128, 4096], BF16))
                    w2T = e2(nc.sbuf_tensor("w2T", [128, 4096], BF16))
                    w1T = e2(nc.sbuf_tensor("w1T", [128, 4096], BF16))
                    gq_st = [e2(nc.sbuf_tensor(f"gq{i}", [128, 512], F32)) for i in range(2)]
                    G_sb = e2(nc.sbuf_tensor("G_sb", [128, 4 * T], BF16))
                    psG = [e2(nc.psum_tensor(f"psG{i}", [128, 512], F32)) for i in range(2)]
                    psO = [e2(nc.psum_tensor(f"psO{i}", [128, 512], F32)) for i in range(2)]

                    B2 = new_block()
                    barA = [(TP, b1_tp), (SP, b1_sp), (VP, b1_vp)]
                    for i, (dst, src) in enumerate(((w0T, w0t_d), (w2T, w2t_d), (w1T, w1t_d))):
                        C.s_wh += 16
                        emit(B2, "gpsimd", (lambda d_, s_: (lambda e: e.dma_start(d_[:], s_)))(dst, src),
                             waits=(barA if i == 0 else ()), inc=SWH, inc_by=16)

                    sil_done = {}; g_done = {}
                    oev_done = [0] * NT
                    for h in range(NH):
                        for jt in range(4):
                            for tb in range(4):
                                i = h * 16 + jt * 4 + tb
                                w = [(SWH, 48), (VP, qt_evac[NT - 1])]
                                if i >= 1:
                                    w.append((SP, sil_done[i - 1]))
                                for dhc in range(2):
                                    emit(B2, "tensor",
                                         (lambda hh, jj, bb, dd, st_, so_: (lambda e: e.matmul(
                                             psG[0][:],
                                             w0T[:, hh * 1024 + dd * 512 + jj * 128: hh * 1024 + dd * 512 + (jj + 1) * 128],
                                             QT[:, (2 * hh + dd) * T + bb * 512: (2 * hh + dd) * T + (bb + 1) * 512],
                                             start=st_, stop=so_)))(h, jt, tb, dhc, dhc == 0, dhc == 1),
                                         waits=(w if dhc == 0 else ()), inc=(TP if dhc == 1 else None))
                                C.tp += 1
                                tp_gq = C.tp
                                w2_ = []
                                if i >= 1:
                                    w2_.append((VP, g_done[i - 1]))
                                for dhc in range(2):
                                    emit(B2, "tensor",
                                         (lambda hh, jj, bb, dd, st_, so_: (lambda e: e.matmul(
                                             psG[1][:],
                                             w2T[:, hh * 1024 + dd * 512 + jj * 128: hh * 1024 + dd * 512 + (jj + 1) * 128],
                                             QT[:, (2 * hh + dd) * T + bb * 512: (2 * hh + dd) * T + (bb + 1) * 512],
                                             start=st_, stop=so_)))(h, jt, tb, dhc, dhc == 0, dhc == 1),
                                         waits=(w2_ if dhc == 0 else ()), inc=(TP if dhc == 1 else None))
                                C.tp += 1
                                tp_hq = C.tp
                                wsl = [(TP, tp_gq)]
                                if i >= 2:
                                    wsl.append((VP, g_done[i - 2]))
                                C.sp += 1
                                emit(B2, "scalar",
                                     (lambda ii: (lambda e: e.activation(gq_st[ii % 2][:], psG[0][:], AF.Silu)))(i),
                                     waits=wsl, inc=SP)
                                sil_done[i] = C.sp
                                C.vp += 1
                                emit(B2, "vector",
                                     (lambda jj, bb, ii: (lambda e: e.tensor_mul(
                                         G_sb[:, jj * T + bb * 512:jj * T + (bb + 1) * 512],
                                         psG[1][:], gq_st[ii % 2][:])))(jt, tb, i),
                                     waits=[(TP, tp_hq), (SP, sil_done[i])], inc=VP)
                                g_done[i] = C.vp
                        vp_gall = C.vp
                        for tt in range(NT):
                            w = [(VP, vp_gall)]
                            if tt >= 2:
                                w.append((VP, oev_done[tt - 2]))
                            for jc in range(4):
                                emit(B2, "tensor",
                                     (lambda hh, tt_, jj, st_, so_: (lambda e: e.matmul(
                                         psO[tt_ % 2][:, 0:256],
                                         G_sb[:, jj * T + tt_ * 128:jj * T + (tt_ + 1) * 128],
                                         w1T[:, hh * 1024 + jj * 256: hh * 1024 + (jj + 1) * 256],
                                         start=st_, stop=so_)))(h, tt, jc, jc == 0, jc == 3),
                                     waits=(w if jc == 0 else ()), inc=(TP if jc == 3 else None))
                            C.tp += 1
                            tp_o = C.tp
                            C.vp += 1
                            emit(B2, "vector",
                                 (lambda hh, tt_: (lambda e: e.tensor_add(
                                     x_sb[:, tt_ * D + hh * DH: tt_ * D + (hh + 1) * DH],
                                     psO[tt_ % 2][:, 0:256],
                                     x_sb[:, tt_ * D + hh * DH: tt_ * D + (hh + 1) * DH])))(h, tt),
                                 waits=[(TP, tp_o)], inc=VP)
                            oev_done[tt] = C.vp

                    if dbg:
                        emit(B2, "gpsimd", lambda e: e.dma_start(dbgQT_d, QT[:]),
                             waits=[(VP, qt_evac[NT - 1])], inc=SDBG, inc_by=16)
                    b2_tp, b2_sp, b2_vp = C.tp, C.sp, C.vp

            with ExitStack() as s3:
                e3 = s3.enter_context
                xln2_st = [e3(nc.sbuf_tensor(f"x2ln{i}", [128, D], BF16)) for i in range(2)]
                redT = e3(nc.sbuf_tensor("redT", [16, T], BF16))
                wgn_pad = e3(nc.sbuf_tensor("wgn_pad", [128, 16], BF16))
                wgsc = e3(nc.sbuf_tensor("wgsc", [8, 20], F32))
                psT3 = [e3(nc.psum_tensor(f"psT3{i}", [128, 1024], BF16)) for i in range(2)]
                psR = e3(nc.psum_tensor("psR", [16, T], F32))
                psL = [e3(nc.psum_tensor(f"psL{i}", [128, 512], F32)) for i in range(2)]

                B3 = new_block()
                barB = [(TP, b2_tp), (SP, b2_sp), (VP, b2_vp)]

                C.sp += 1
                emit(B3, "scalar", lambda e: e.activation(wgsc[:, 0:16], wg_sb[:], AF.Square),
                     waits=barB + [(SSM, C.s_small)], inc=SP)
                sp_wg0 = C.sp
                C.vp += 1
                emit(B3, "vector", lambda e: e.reduce_sum(wgsc[:, 16:17], wgsc[:, 0:16], axis=AX.X),
                     waits=barB + [(SP, sp_wg0)], inc=VP)
                vp_wg1 = C.vp
                C.sp += 1
                emit(B3, "scalar", lambda e: e.activation(wgsc[:, 17:18], wgsc[:, 16:17], AF.Sqrt),
                     waits=[(VP, vp_wg1)], inc=SP)
                sp_wg1 = C.sp
                C.vp += 1
                emit(B3, "vector", lambda e: e.tensor_scalar(wgsc[:, 18:19], wgsc[:, 17:18], 1.0, 1e-4, ALU.mult, ALU.max),
                     waits=[(SP, sp_wg1)], inc=VP)
                v_wg3 = C.vp
                C.sp += 1
                emit(B3, "scalar", lambda e: e.activation(wgsc[:, 19:20], wgsc[:, 18:19], AF.Identity),
                     waits=[(VP, v_wg3)], inc=SP)
                sp_wg4 = C.sp
                C.vp += 1
                emit(B3, "vector", lambda e: e.reciprocal(wgsc[:, 16:17], wgsc[:, 19:20]),
                     waits=[(SP, sp_wg4)], inc=VP)
                v_wg5 = C.vp
                C.vp += 1
                emit(B3, "vector", lambda e: e.memset(wgn_pad[:], 0.0), waits=barB, inc=VP)
                vp_wgall = C.vp
                C.sp += 1
                emit(B3, "scalar", lambda e: e.activation(wgn_pad[0:8, :], wg_sb[:], AF.Identity, scale=wgsc[:, 16:17]),
                     waits=[(VP, vp_wgall), (VP, v_wg5)], inc=SP)
                sp_wgn = C.sp
                emit(B3, "tensor", lambda e: e.transpose(psT3[0][0:16, 0:128], wgn_pad[:], ident[:]),
                     waits=barB + [(SP, sp_wgn), (SSM, C.s_small)], inc=TP)
                C.tp += 1
                tp_wgt = C.tp
                C.sp += 1
                emit(B3, "scalar", lambda e: e.copy(wgnT[:], psT3[0][0:16, 0:8]), waits=[(TP, tp_wgt)], inc=SP)
                sp_wgnT = C.sp

                x1t_done = [0] * NT
                ssq2_done = [0] * NT
                for t in range(NT):
                    a = t % 2
                    extra = barB if t == 0 else []
                    xws = list(extra)
                    if t >= 1:
                        xws.append((VP, ssq2_done[t - 1]))
                    ln_sp, ssq2_done[t] = ln_stats(B3, x_sb, t * D, xln2_st[a], 1e-5,
                                     w_scalar=xws, w_vector=list(extra))
                    w = [(SP, ln_sp)]
                    if t == 0:
                        w.append((SP, sp_wgnT))
                    if t >= 2:
                        w.append((VP, x1t_done[t - 2]))
                    for k in range(ND):
                        emit(B3, "tensor",
                             (lambda aa, kk: (lambda e: e.transpose(psT3[aa][:, kk * 128:(kk + 1) * 128],
                                                                    xln2_st[aa][:, kk * 128:(kk + 1) * 128], ident[:])))(a, k),
                             waits=(w if k == 0 else ()), inc=(TP if k == ND - 1 else None))
                    C.tp += 1
                    tpv = C.tp
                    C.vp += 1
                    emit(B3, "vector",
                         (lambda aa, tt: (lambda e: e.tensor_copy(
                             X1T[:].rearrange("p (i c) -> p i c", i=ND)[:, :, tt * 128:(tt + 1) * 128],
                             psT3[aa][:].rearrange("p (i c) -> p i c", i=ND))))(a, t),
                         waits=[(TP, tpv)], inc=VP)
                    x1t_done[t] = C.vp
                vp_x1t_all = C.vp

                for k in range(ND):
                    for tb in range(4):
                        last = (k == ND - 1 and tb == 3)
                        emit(B3, "tensor",
                             (lambda kk, bb, st_, so_: (lambda e: e.matmul(
                                 psR[:, bb * 512:(bb + 1) * 512],
                                 wred_bf[:, kk * 16:(kk + 1) * 16],
                                 X1T[:, kk * T + bb * 512: kk * T + (bb + 1) * 512],
                                 start=st_, stop=so_)))(k, tb, k == 0, k == ND - 1),
                             waits=([(VP, vp_x1t_all), (SWQ, 32)] if (k == 0 and tb == 0) else ()),
                             inc=(TP if last else None))
                C.tp += 1
                tp_red = C.tp
                C.sp += 1
                emit(B3, "scalar", lambda e: e.copy(redT[:], psR[:]), waits=[(TP, tp_red)], inc=SP)
                sp_redbf = C.sp

                lg_done = [0] * NT
                d8_done = [0] * NT
                for t in range(NT):
                    a = t % 2
                    w = [(SP, sp_redbf)]
                    if t >= 2:
                        w.append((SP, lg_done[t - 2]))
                    emit(B3, "tensor",
                         (lambda aa, tt: (lambda e: e.matmul(
                             psL[aa][:, 0:8], redT[:, tt * 128:(tt + 1) * 128], wgnT[:],
                             start=True, stop=True)))(a, t),
                         waits=w, inc=TP)
                    C.tp += 1
                    tp_lg = C.tp
                    wlg = [(TP, tp_lg)]
                    if t >= 2:
                        wlg.append((VP, d8_done[t - 2]))
                    C.sp += 1
                    emit(B3, "scalar", (lambda aa: (lambda e: e.copy(lgS[:, aa * 8:aa * 8 + 8], psL[aa][:, 0:8])))(a),
                         waits=wlg, inc=SP)
                    lg_done[t] = C.sp
                    lgsl = lgS[:, a * 8:a * 8 + 8]
                    C.vp += 1
                    emit(B3, "vector", (lambda ls: (lambda e: e.reduce_max(col(S_NMX), ls, axis=AX.X, negate=True)))(lgsl),
                         waits=[(SP, lg_done[t])], inc=VP)
                    v_nmx = C.vp
                    d8_done[t] = C.vp
                    C.sp += 1
                    emit(B3, "scalar", (lambda ls: (lambda e: e.activation(routS[:, 0:8], ls, AF.Identity, bias=col(S_NMX))))(lgsl),
                         waits=[(VP, v_nmx), (SP, lg_done[t])], inc=SP)
                    sp_d8 = C.sp
                    C.sp += 1
                    emit(B3, "scalar", lambda e: e.sign(routS[:, 8:16], routS[:, 0:8]),
                         waits=[(SP, sp_d8)], inc=SP)
                    sp_sg = C.sp
                    C.sp += 1
                    emit(B3, "scalar", (lambda ls: (lambda e: e.activation(routS[:, 16:24], ls, AF.Exp, bias=col(S_NMX))))(lgsl),
                         inc=SP)
                    sp_sx = C.sp
                    C.vp += 1
                    emit(B3, "vector", lambda e: e.reduce_sum(col(S_ESUM), routS[:, 16:24], axis=AX.X),
                         waits=[(SP, sp_sx)], inc=VP)
                    v_es = C.vp
                    C.sp += 1
                    emit(B3, "scalar", lambda e: e.activation(col(S_REC), col(S_ESUM), AF.Identity),
                         waits=[(VP, v_es)], inc=SP)
                    sp_esc = C.sp
                    C.vp += 1
                    emit(B3, "vector", lambda e: e.reciprocal(col(S_REC2), col(S_REC)),
                         waits=[(SP, sp_esc)], inc=VP)
                    v_rec = C.vp
                    C.sp += 1
                    emit(B3, "scalar", (lambda tt: (lambda e: e.activation(
                        c_sb[:, tt * E:(tt + 1) * E], routS[:, 8:16], AF.Identity,
                        scale=col(S_REC2), bias=col(S_REC2))))(t),
                        waits=[(VP, v_rec), (SP, sp_sg)], inc=SP)
                b3_c_sp = C.sp
                b3_c_vp = C.vp

                for g in range(4):
                    dst = out_d[g * 512:(g + 1) * 512, :].rearrange("(n p) d -> p n d", p=128)
                    src = x_sb[:, g * 4096:(g + 1) * 4096].rearrange("p (n d) -> p n d", n=4)
                    C.s_base += 16
                    emit(B3, "sync", (lambda d_, s_: (lambda e: e.dma_start(d_, s_)))(dst, src),
                         waits=(barB if g == 0 else ()), inc=SB, inc_by=16)

                if dbg:
                    emit(B3, "gpsimd", lambda e: e.dma_start(dbgX1T_d, X1T[:]),
                         waits=[(VP, vp_x1t_all)], inc=SDBG, inc_by=16)
                    emit(B3, "gpsimd", lambda e: e.dma_start(dbgRED_d, redT[:]),
                         waits=[(SP, sp_redbf)], inc=SDBG, inc_by=16)
                    emit(B3, "gpsimd", lambda e: e.dma_start(dbgC_d, c_sb[:]),
                         waits=[(SP, b3_c_sp)], inc=SDBG, inc_by=16)
                if dbg & 2:
                    emit(B3, "gpsimd", lambda e: e.nop(),
                         waits=[(SB, C.s_base), (SDBG, 64)])
                b3_tp, b3_sp, b3_vp = C.tp, C.sp, C.vp

        if dbg & 2:
            check_deadlock()
            for bb in blocks:
                with nc.Block() as blk:
                    for eng_name in ENGS:
                        ops = bb[eng_name]
                        if not ops:
                            continue

                        def runner(ops_):
                            def _(eng):
                                for f in ops_:
                                    f(eng)
                            return _
                        getattr(blk, eng_name)(runner(ops))
            return nc
        with ExitStack() as s4:
            e4 = s4.enter_context
            HT = e4(nc.sbuf_tensor("HT", [128, NFC * T], BF16))
            w1sl = [e4(nc.sbuf_tensor(f"w1sl{i}", [128, 4096], BF16)) for i in range(2)]
            w2buf = [e4(nc.sbuf_tensor(f"w2buf{i}", [128, NFC * D], BF16)) for i in range(2)]
            ytmp = [e4(nc.sbuf_tensor(f"ytmp{i}", [128, D], F32)) for i in range(2)]
            b2bf = e4(nc.sbuf_tensor("b2bf", [1, E * D], BF16))
            psH = [e4(nc.psum_tensor(f"psH{i}", [128, 512], F32)) for i in range(2)]
            psY = [e4(nc.psum_tensor(f"psY{i}", [128, 1024], F32)) for i in range(2)]

            B4 = new_block()
            barC = [(TP, b3_tp), (SP, b3_sp), (VP, b3_vp), (SB, C.s_base)]

            units = [(e_, hf_) for e_ in range(E) for hf_ in range(2)]

            w1_slot_uses = {0: [], 1: []}
            w1_slab_ready = {}
            w2_ready = {}
            w1_consumed_tp = {}
            w2_consumed_tp = {}
            slab_idx = 0
            for ui, (e_, hf_) in enumerate(units):
                for s_ in range(4):
                    slot = slab_idx % 2
                    w1_slot_uses[slot].append((e_, hf_, s_))
                    C.s_w1[slot] += 32
                    w1_slab_ready[(e_, hf_, s_)] = (slot, C.s_w1[slot])
                    slab_idx += 1
                C.s_w2[ui % 2] += 64
                w2_ready[(e_, hf_)] = C.s_w2[ui % 2]

            gelu_hist = []
            gelu_done = {}
            ytmp_done = {}
            acc_k = [0, 0]
            accums = []

            for ui, (e_, hf_) in enumerate(units):
                for s_ in range(4):
                    slot, w1cnt = w1_slab_ready[(e_, hf_, s_)]
                    for ft in range(4):
                        fcl = s_ * 4 + ft
                        for tb in range(4):
                            gi = len(gelu_hist)
                            aa = gi % 2
                            w = []
                            if ft == 0 and tb == 0:
                                w.append((SW1[slot], w1cnt))
                                if ui == 0 and s_ == 0:
                                    w += barC
                            if gi >= 2:
                                w.append((SP, gelu_hist[gi - 2]))
                            for k in range(ND):
                                emit(B4, "tensor",
                                     (lambda aa_, slt, kk, ff, bb, st_, so_: (lambda e: e.matmul(
                                         psH[aa_][:],
                                         w1sl[slt][:, kk * 512 + ff * 128: kk * 512 + (ff + 1) * 128],
                                         X1T[:, kk * T + bb * 512: kk * T + (bb + 1) * 512],
                                         start=st_, stop=so_)))(aa, slot, k, ft, tb, k == 0, k == ND - 1),
                                     waits=(w if k == 0 else ()), inc=(TP if k == ND - 1 else None))
                            C.tp += 1
                            tp_g = C.tp
                            wg_ = [(TP, tp_g)]
                            if ui == 0 and fcl == 0 and tb == 0:
                                wg_.append((SB, C.s_base))
                            C.sp += 1
                            emit(B4, "scalar",
                                 (lambda aa_, ee, hh, ff, bb: (lambda e: e.activation(
                                     HT[:, ff * T + bb * 512: ff * T + (bb + 1) * 512],
                                     psH[aa_][:], AF.Gelu,
                                     bias=b1c[:, ee * 32 + hh * 16 + ff: ee * 32 + hh * 16 + ff + 1])))(aa, e_, hf_, fcl, tb),
                                 waits=wg_, inc=SP)
                            gelu_hist.append(C.sp)
                    w1_consumed_tp[(e_, hf_, s_)] = C.tp
                gelu_done[(e_, hf_)] = C.sp

                for tc in range(NT):
                    b = tc % 2
                    w = [(SP, gelu_done[(e_, hf_)]), (SW2[ui % 2], w2_ready[(e_, hf_)])]
                    key2 = (e_, hf_, tc - 2)
                    if key2 in ytmp_done:
                        w.append((SP, ytmp_done[key2]))
                    elif ui > 0:
                        pe, ph = units[ui - 1]
                        w.append((SP, ytmp_done[(pe, ph, NT - 2 + tc)]))
                    if hf_ == 0:
                        for dh in range(2):
                            emit(B4, "tensor",
                                 (lambda bb, ee, dd: (lambda e: e.matmul(
                                     psY[bb][:, dd * 512:(dd + 1) * 512],
                                     ones1[0:1, :], b2bf[0:1, ee * D + dd * 512: ee * D + (dd + 1) * 512],
                                     start=True, stop=False)))(b, e_, dh),
                                 waits=((w + [(SB2, 16)]) if dh == 0 else ()))
                    for fcl in range(NFC):
                        for dh in range(2):
                            last_all = (fcl == NFC - 1 and dh == 1)
                            emit(B4, "tensor",
                                 (lambda bb, ff, tt, dd, st__, so__, wb: (lambda e: e.matmul(
                                     psY[bb][:, dd * 512:(dd + 1) * 512],
                                     HT[:, ff * T + tt * 128: ff * T + (tt + 1) * 128],
                                     w2buf[wb][:, ff * D + dd * 512: ff * D + (dd + 1) * 512],
                                     start=st__, stop=so__)))(b, fcl, tc, dh,
                                                              (hf_ == 1 and fcl == 0), fcl == NFC - 1, ui % 2),
                                 waits=(w if (hf_ == 1 and fcl == 0 and dh == 0) else ()),
                                 inc=(TP if last_all else None))
                    C.tp += 1
                    tp_y = C.tp
                    par = tc % 2
                    kclass = acc_k[par]
                    wy = [(TP, tp_y), (SP, b3_c_sp)]
                    if kclass >= 1:
                        wy.append((SAC[par], 16 * kclass))
                    C.sp += 1
                    emit(B4, "scalar",
                         (lambda bb, ee, tt: (lambda e: e.activation(
                             ytmp[tt % 2][:], psY[bb][:], AF.Identity,
                             scale=c_sb[:, tt * E + ee: tt * E + ee + 1])))(b, e_, tc),
                         waits=wy, inc=SP)
                    ytmp_done[(e_, hf_, tc)] = C.sp
                    acc_k[par] += 1
                    accums.append((e_, hf_, tc, par, acc_k[par], C.sp))
                w2_consumed_tp[(e_, hf_)] = C.tp

            def emit_w1(e_, hf_, s_, slot, first_waits=()):
                uses = w1_slot_uses[slot]
                myi = uses.index((e_, hf_, s_))
                w = list(first_waits)
                if myi >= 1:
                    w.append((TP, w1_consumed_tp[uses[myi - 1]]))
                fc0 = hf_ * FH + s_ * 512
                for g in range(2):
                    src = w1_d[e_, g * 512:(g + 1) * 512, fc0:fc0 + 512].rearrange("(c p) f -> p c f", p=128)
                    dst = w1sl[slot][:, g * 2048:(g + 1) * 2048].rearrange("p (c f) -> p c f", c=4)
                    emit(B4, "gpsimd", (lambda d_, s__: (lambda e: e.dma_start(d_, s__)))(dst, src),
                         waits=(w if g == 0 else ()), inc=SW1[slot], inc_by=16)

            def emit_w2(e_, hf_, r, first_waits=()):
                ui_ = e_ * 2 + hf_
                w = list(first_waits)
                if ui_ >= 2:
                    w.append((TP, w2_consumed_tp[units[ui_ - 2]]))
                for g in range(4):
                    src = w2_d[e_, hf_ * FH + g * 512: hf_ * FH + (g + 1) * 512, :].rearrange("(c p) d -> p c d", p=128)
                    dst = w2buf[r][:, g * 4096:(g + 1) * 4096].rearrange("p (c d) -> p c d", c=4)
                    emit(B4, "gpsimd", (lambda d_, s__: (lambda e: e.dma_start(d_, s__)))(dst, src),
                         waits=(w if g == 0 else ()), inc=SW2[r], inc_by=16)

            emit(B4, "gpsimd", lambda e: e.dma_start(b2bf[:], b2r_d), waits=barC, inc=SB2, inc_by=16)
            emit_w1(0, 0, 0, w1_slab_ready[(0, 0, 0)][0])
            emit_w1(0, 0, 1, w1_slab_ready[(0, 0, 1)][0])
            emit_w2(0, 0, 0)
            emit_w1(0, 0, 2, w1_slab_ready[(0, 0, 2)][0])
            emit_w1(0, 0, 3, w1_slab_ready[(0, 0, 3)][0])

            acc_by_unit = {}
            for a_ in accums:
                acc_by_unit.setdefault((a_[0], a_[1]), []).append(a_)

            def emit_acc(a_):
                ea, ha, tca, para, kcla, spva = a_
                w = [(SP, spva)]
                if ea == 0 and ha == 0 and tca == 0:
                    w.append((SB, C.s_base))
                emit(B4, "gpsimd",
                     (lambda tt: (lambda e: e.dma_start(
                         out_d[tt * 128:(tt + 1) * 128, :], ytmp[tt % 2][:], accum_op=ALU.add)))(tca),
                     waits=w, inc=SAC[para], inc_by=16)

            for ui, (e_, hf_) in enumerate(units):
                aops = acc_by_unit[(e_, hf_)]
                early = []
                late = []
                if ui + 1 < len(units):
                    ne, nh = units[ui + 1]
                    early = [("w1", (ne, nh, 0, w1_slab_ready[(ne, nh, 0)][0])),
                             ("w1", (ne, nh, 1, w1_slab_ready[(ne, nh, 1)][0])),
                             ("w2", (ne, nh, (ui + 1) % 2))]
                    late = [("w1", (ne, nh, 2, w1_slab_ready[(ne, nh, 2)][0])),
                            ("w1", (ne, nh, 3, w1_slab_ready[(ne, nh, 3)][0]))]
                for i, a_ in enumerate(aops):
                    if i < len(early):
                        kind, pl = early[i]
                        (emit_w1 if kind == "w1" else emit_w2)(*pl)
                    emit_acc(a_)
                for kind, pl in late:
                    (emit_w1 if kind == "w1" else emit_w2)(*pl)

            emit(B4, "gpsimd", lambda e: e.nop(),
                 waits=[(SAC[0], 16 * acc_k[0]), (SAC[1], 16 * acc_k[1])] +
                       ([(SDBG, 64)] if dbg else []))

            check_deadlock()

            for bb in blocks:
                with nc.Block() as blk:
                    for eng_name in ENGS:
                        ops = bb[eng_name]
                        if not ops:
                            continue

                        def runner(ops_):
                            def _(eng):
                                for f in ops_:
                                    f(eng)
                            return _
                        getattr(blk, eng_name)(runner(ops))
    return nc


def kernel(**inputs):
    x = np.asarray(inputs["L0_attn_hidden_states"], dtype=np.float32)
    qkv_w = np.asarray(inputs["L0_attn_qkv_w"], dtype=np.float32)
    q_norm_w = np.asarray(inputs["L0_attn_q_norm_w"], dtype=np.float32)
    w0 = np.asarray(inputs["L0_attn_w0"], dtype=np.float32)
    w1a = np.asarray(inputs["L0_attn_w1"], dtype=np.float32)
    w2a = np.asarray(inputs["L0_attn_w2"], dtype=np.float32)
    w_red = np.asarray(inputs["L0_ffn_w_red"], dtype=np.float32)
    wg = np.asarray(inputs["L0_ffn_wg"], dtype=np.float32)
    W1 = np.asarray(inputs["L0_ffn_w1"], dtype=np.float32)
    b1 = np.asarray(inputs["L0_ffn_b1"], dtype=np.float32)
    W2 = np.asarray(inputs["L0_ffn_w2"], dtype=np.float32)
    b2 = np.asarray(inputs["L0_ffn_b2"], dtype=np.float32)

    Wq = qkv_w[:D]
    wqt = np.ascontiguousarray(Wq.T.reshape(ND, 128, D).transpose(1, 0, 2).reshape(128, ND * D))
    w0t = np.ascontiguousarray(np.stack([w0[h].T.reshape(2, 128, J) for h in range(NH)])
                               .transpose(2, 0, 1, 3).reshape(128, 4096))
    w2t = np.ascontiguousarray(np.stack([w2a[h].T.reshape(2, 128, J) for h in range(NH)])
                               .transpose(2, 0, 1, 3).reshape(128, 4096))
    w1t = np.ascontiguousarray(np.stack([w1a[h].T.reshape(4, 128, DH) for h in range(NH)])
                               .transpose(2, 0, 1, 3).reshape(128, 4096))
    wredl = np.ascontiguousarray(w_red.reshape(ND, 128, 16).transpose(1, 0, 2).reshape(128, 128))
    qnw_rep = np.ascontiguousarray(np.broadcast_to(q_norm_w, (128, D)))
    b1c = np.ascontiguousarray(b1.reshape(E, 32, 128).transpose(2, 0, 1).reshape(128, 256))
    b2r = np.ascontiguousarray(b2.reshape(1, E * D))
    ident = np.eye(128, dtype=ml_dtypes.bfloat16)
    ones1 = np.ones((1, 128), dtype=ml_dtypes.bfloat16)

    nc = build_nc(dbg=_DBG[0])
    common = dict(wqt=wqt, w0t=w0t, w2t=w2t, w1t=w1t, wred=wredl, wg=wg, qnw=qnw_rep,
                  b1c=b1c, b2r=b2r, ident=ident, ones1=ones1, w1e=W1, w2e=W2)
    in_maps = [dict(common, x=np.ascontiguousarray(x[i])) for i in range(8)]
    res = run_bass_kernel_spmd(nc, in_maps, core_ids=list(range(8)))
    if _DBG[0]:
        _DBG[1] = res.results
    return np.stack([r["out"] for r in res.results]).astype(np.float32)


_DBG = [0, None]


# version 09
# speedup vs baseline: 1.9817x; 1.9700x over previous
import numpy as np
import ml_dtypes
from contextlib import ExitStack

import concourse.bass as bass
import concourse.mybir as mybir
from concourse.bass_utils import run_bass_kernel_spmd

F32 = mybir.dt.float32
BF16 = mybir.dt.bfloat16
AF = mybir.ActivationFunctionType
ALU = mybir.AluOpType
AX = mybir.AxisListType

T = 2048
NT = 16
D = 1024
ND = 8
NH = 4
DH = 256
J = 512
F = 4096
E = 8
FH = 2048
NFC = 16

ENGS = ("sync", "tensor", "scalar", "vector", "gpsimd")
SPARSE = True
CAP = 512
NSLOT = E * CAP


def build_nc(dbg=0):
    nc = bass.Bass("TRN2", target_bir_lowering=False)

    x_d = nc.dram_tensor("x", [T, D], F32, kind="ExternalInput").ap()
    wqt_d = nc.dram_tensor("wqt", [128, ND * D], F32, kind="ExternalInput").ap()
    w0t_d = nc.dram_tensor("w0t", [128, 4096], F32, kind="ExternalInput").ap()
    w2t_d = nc.dram_tensor("w2t", [128, 4096], F32, kind="ExternalInput").ap()
    w1t_d = nc.dram_tensor("w1t", [128, 4096], F32, kind="ExternalInput").ap()
    wred_d = nc.dram_tensor("wred", [128, 128], F32, kind="ExternalInput").ap()
    wg_d = nc.dram_tensor("wg", [8, 16], F32, kind="ExternalInput").ap()
    qnw_d = nc.dram_tensor("qnw", [128, D], F32, kind="ExternalInput").ap()
    b1c_d = nc.dram_tensor("b1c", [128, 256], F32, kind="ExternalInput").ap()
    b2r_d = nc.dram_tensor("b2r", [1, E * D], F32, kind="ExternalInput").ap()
    ident_d = nc.dram_tensor("ident", [128, 128], BF16, kind="ExternalInput").ap()
    ones_d = nc.dram_tensor("ones1", [1, 128], BF16, kind="ExternalInput").ap()
    w1_d = nc.dram_tensor("w1e", [E, D, F], F32, kind="ExternalInput").ap()
    w2_d = nc.dram_tensor("w2e", [E, F, D], F32, kind="ExternalInput").ap()
    out_d = nc.dram_tensor("out", [T, D], F32, kind="ExternalOutput").ap()
    if SPARSE:
        slotX_d = nc.dram_tensor("slotX", [NSLOT, D], BF16).ap()
        slotY_d = nc.dram_tensor("slotY", [NSLOT, D], F32).ap()
        rowA_d = nc.dram_tensor("rowA", [1, T], F32).ap()
        rowB_d = nc.dram_tensor("rowB", [1, T], F32).ap()
    if dbg:
        dbgQT_d = nc.dram_tensor("dbgQT", [128, ND * T], BF16, kind="ExternalOutput").ap()
        dbgX1T_d = nc.dram_tensor("dbgX1T", [128, ND * T], BF16, kind="ExternalOutput").ap()
        dbgC_d = nc.dram_tensor("dbgC", [128, NT * E], F32, kind="ExternalOutput").ap()
        dbgRED_d = nc.dram_tensor("dbgRED", [16, T], BF16, kind="ExternalOutput").ap()

    blocks = []

    def new_block():
        b = {e: [] for e in ENGS}
        blocks.append(b)
        return b

    OPS = []
    sem_hist = {}
    sem_cum = {}

    SEMNAMES = {}

    def emit(blk, eng, make, waits=(), inc=None, inc_by=1):
        wl = [(h, v) for (h, v) in waits]
        for (h, v) in wl:
            SEMNAMES.setdefault(id(h), getattr(h, 'name', None) or str(h))

        def fn(e):
            for (h, v) in wl:
                e.wait_ge(h, v)
            ins = make(e)
            if inc is not None:
                ins.then_inc(inc, inc_by)
        blk[eng].append(fn)
        oi = len(OPS)
        rec = dict(eng=eng, waits=[(id(h), v) for (h, v) in wl], inc=None, oi=oi)
        if inc is not None:
            k = id(inc)
            sem_cum[k] = sem_cum.get(k, 0) + inc_by
            sem_hist.setdefault(k, []).append((sem_cum[k], oi))
            rec["inc"] = (k, sem_cum[k])
        OPS.append(rec)

    def check_deadlock():
        n = len(OPS)
        adj = [[] for _ in range(n)]
        indeg = [0] * n
        last_on = {}
        for op in OPS:
            oi = op["oi"]
            if op["eng"] in last_on:
                adj[last_on[op["eng"]]].append(oi)
                indeg[oi] += 1
            last_on[op["eng"]] = oi
            for (k, v) in op["waits"]:
                hist = sem_hist.get(k, [])
                ach = None
                for (cum, j) in hist:
                    if cum >= v:
                        ach = j
                        break
                if ach is None:
                    raise AssertionError(f"op {oi} ({op['eng']}) waits sem value {v} never reached")
                adj[ach].append(oi)
                indeg[oi] += 1
        from collections import deque
        dq = deque(i for i in range(n) if indeg[i] == 0)
        seen = 0
        while dq:
            u = dq.popleft()
            seen += 1
            for v2 in adj[u]:
                indeg[v2] -= 1
                if indeg[v2] == 0:
                    dq.append(v2)
        if seen != n:
            stuck = set(i for i in range(n) if indeg[i] > 0)
            color = {}
            path = []

            def dfs(u):
                color[u] = 1
                path.append(u)
                for v2 in adj[u]:
                    if v2 not in stuck:
                        continue
                    if color.get(v2) == 1:
                        return path[path.index(v2):]
                    if v2 not in color:
                        r = dfs(v2)
                        if r:
                            return r
                color[u] = 2
                path.pop()
                return None
            cyc = None
            for i in sorted(stuck):
                if i not in color:
                    cyc = dfs(i)
                    if cyc:
                        break
            detail = [(i, OPS[i]["eng"], [(SEMNAMES.get(k, k), v) for (k, v) in OPS[i]["waits"]]) for i in (cyc or [])]
            raise AssertionError(f"deadlock cycle: {detail}")

    class C:
        tp = 0; sp = 0; vp = 0
        s_x = 0; s_small = 0; s_base = 0; s_wq = 0; s_wh = 0; s_b2 = 0
        s_w1 = [0, 0]; s_w2 = [0, 0]; s_acc = [0, 0]

    with ExitStack() as st:
        en = st.enter_context
        TP = en(nc.semaphore("TP")); SP = en(nc.semaphore("SP")); VP = en(nc.semaphore("VP"))
        SX = en(nc.semaphore("SX")); SSM = en(nc.semaphore("SSM")); SB = en(nc.semaphore("SBASE"))
        SWQ = en(nc.semaphore("SWQ")); SWH = en(nc.semaphore("SWH")); SB2 = en(nc.semaphore("SB2"))
        SW1 = [en(nc.semaphore("SW1a")), en(nc.semaphore("SW1b"))]
        SW2 = [en(nc.semaphore("SW2a")), en(nc.semaphore("SW2b"))]
        SAC = [en(nc.semaphore("SACa")), en(nc.semaphore("SACb"))]
        SDBG = en(nc.semaphore("SDBG"))
        SRP = en(nc.semaphore("SRP"))
        SSC = en(nc.semaphore("SSC"))
        SXT = [en(nc.semaphore("SXTa")), en(nc.semaphore("SXTb"))]
        SYS = [en(nc.semaphore("SYSa")), en(nc.semaphore("SYSb"))]
        SGA = [en(nc.semaphore("SGAa")), en(nc.semaphore("SGAb"))]
        SZF = en(nc.semaphore("SZF"))
        GPC = en(nc.semaphore("GPC"))

        X1T = en(nc.sbuf_tensor("X1T", [128, ND * T], BF16))
        c_sb = en(nc.sbuf_tensor("c_sb", [128, NT * E], F32))
        ident = en(nc.sbuf_tensor("identb", [128, 128], BF16))
        ones1 = en(nc.sbuf_tensor("ones1b", [1, 128], BF16))
        b1c = en(nc.sbuf_tensor("b1cb", [128, 256], F32))
        wg_sb = en(nc.sbuf_tensor("wg_sb", [8, 16], F32))
        wred_bf = en(nc.sbuf_tensor("wred_bf", [128, 128], BF16))
        wgnT = en(nc.sbuf_tensor("wgnT", [16, 8], BF16))
        cols = en(nc.sbuf_tensor("cols", [128, 24], F32))
        lgS = en(nc.sbuf_tensor("lgS", [128, 16], F32))
        routS = en(nc.sbuf_tensor("routS", [128, 24], F32))
        idxf = en(nc.sbuf_tensor("idxf", [128, 16], F32))
        scoreS = en(nc.sbuf_tensor("scoreS", [128, 16], F32))
        islotI = en(nc.sbuf_tensor("islotI", [128, 16], mybir.dt.int32))
        mxi8 = en(nc.sbuf_tensor("mxi8", [128, 8], F32))
        idx8u = en(nc.sbuf_tensor("idx8u", [128, 8], mybir.dt.uint32))

        (S_SUM, S_SSQ, S_M, S_E2, S_MSQ, S_VAR, S_SD, S_RS, S_NB,
         S_SS2, S_SD2, S_RS2, S_NMX, S_ESUM, S_REC, S_REC2, S_SD2B) = range(17)

        def col(i):
            return cols[:, i:i + 1]

        def ln_stats(B, src_sb, off, dst_stage, eps, w_scalar=(), w_vector=()):
            xsl = src_sb[:, off:off + D]
            C.sp += 1
            emit(B, "scalar", (lambda xs: (lambda e: e.activation(sqscr[:], xs, AF.Square)))(xsl),
                 waits=list(w_scalar), inc=SP)
            sp_sq = C.sp
            C.vp += 1
            emit(B, "vector", (lambda xs: (lambda e: e.reduce_sum(col(S_SUM), xs, axis=AX.X)))(xsl),
                 waits=list(w_vector), inc=VP)
            v_sum = C.vp
            C.vp += 1
            emit(B, "vector", lambda e: e.reduce_sum(col(S_SSQ), sqscr[:], axis=AX.X),
                 waits=[(SP, sp_sq)], inc=VP)
            v_ssq = C.vp
            C.vp += 1
            emit(B, "vector", lambda e: e.tensor_scalar(col(S_M), col(S_SUM), 1.0 / D, 0.0, ALU.mult, ALU.add),
                 waits=[(VP, v_sum)], inc=VP)
            v_m = C.vp
            C.vp += 1
            emit(B, "vector", (lambda ep: (lambda e: e.tensor_scalar(col(S_E2), col(S_SSQ), 1.0 / D, ep, ALU.mult, ALU.add)))(eps),
                 waits=[(VP, v_ssq)], inc=VP)
            C.vp += 1
            emit(B, "vector", lambda e: e.tensor_mul(col(S_MSQ), col(S_M), col(S_M)),
                 waits=[(VP, v_m)], inc=VP)
            v_msq = C.vp
            C.vp += 1
            emit(B, "vector", lambda e: e.tensor_scalar(col(S_VAR), col(S_MSQ), -1.0, col(S_E2), ALU.mult, ALU.add),
                 waits=[(VP, v_msq)], inc=VP)
            vp_var = C.vp
            C.sp += 1
            emit(B, "scalar", lambda e: e.activation(col(S_SD), col(S_VAR), AF.Sqrt),
                 waits=[(VP, vp_var)], inc=SP)
            sp_sd = C.sp
            C.vp += 1
            emit(B, "vector", lambda e: e.reciprocal(col(S_RS), col(S_SD)), waits=[(SP, sp_sd)], inc=VP)
            v_rs = C.vp
            C.vp += 1
            emit(B, "vector", lambda e: e.tensor_scalar(col(S_NB), col(S_M), col(S_RS), -1.0, ALU.mult, ALU.mult),
                 waits=[(VP, v_rs)], inc=VP)
            vp_nb = C.vp
            C.sp += 1
            emit(B, "scalar", (lambda xs, dst: (lambda e: e.activation(dst, xs, AF.Identity, bias=col(S_NB), scale=col(S_RS))))(xsl, dst_stage),
                 waits=[(VP, vp_nb)], inc=SP)
            return C.sp, v_ssq

        with ExitStack() as sxs:
            ex_ = sxs.enter_context
            x_sb = ex_(nc.sbuf_tensor("x_sb", [128, NT * D], F32))
            sqscr = ex_(nc.sbuf_tensor("sqscr", [128, D], F32))

            with ExitStack() as sqt:
                eq_ = sqt.enter_context
                QT = eq_(nc.sbuf_tensor("QT", [128, ND * T], BF16))

                with ExitStack() as s1:
                    e1 = s1.enter_context
                    WqT = e1(nc.sbuf_tensor("WqT", [128, ND * D], BF16))
                    qnw = e1(nc.sbuf_tensor("qnw_sb", [128, D], F32))
                    xln_st = [e1(nc.sbuf_tensor(f"xln{i}", [128, D], BF16)) for i in range(2)]
                    xt0_st = [e1(nc.sbuf_tensor(f"xt0{i}", [128, D], BF16)) for i in range(2)]
                    qn_st = [e1(nc.sbuf_tensor(f"qn{i}", [128, D], BF16)) for i in range(2)]
                    qw_sb = e1(nc.sbuf_tensor("qw", [128, D], F32))
                    qw2 = e1(nc.sbuf_tensor("qw2", [128, D], F32))
                    h4 = e1(nc.sbuf_tensor("h4", [128, 16], F32))
                    psT = [e1(nc.psum_tensor(f"psT{i}", [128, 1024], BF16)) for i in range(2)]
                    psT2 = [e1(nc.psum_tensor(f"psT2{i}", [128, 1024], BF16)) for i in range(2)]
                    psQ = [e1(nc.psum_tensor(f"psQ{i}", [128, 1024], F32)) for i in range(2)]

                    B1 = new_block()

                    for g in range(4):
                        src = x_d[g * 512:(g + 1) * 512, :].rearrange("(n p) d -> p n d", p=128)
                        dst = x_sb[:, g * 4096:(g + 1) * 4096].rearrange("p (n d) -> p n d", n=4)
                        C.s_x += 16
                        emit(B1, "sync", (lambda d_, s_: (lambda e: e.dma_start(d_, s_)))(dst, src),
                             inc=SX, inc_by=16)
                    for (dst, src) in ((qnw[:], qnw_d), (b1c[:], b1c_d), (wg_sb[:], wg_d),
                                       (ident[:], ident_d), (ones1[:], ones_d)):
                        C.s_small += 16
                        emit(B1, "sync", (lambda d_, s_: (lambda e: e.dma_start(d_, s_)))(dst, src),
                             inc=SSM, inc_by=16)
                    C.s_wq += 16
                    emit(B1, "gpsimd", lambda e: e.dma_start(WqT[:], wqt_d), inc=SWQ, inc_by=16)
                    C.s_wq += 16
                    emit(B1, "gpsimd", lambda e: e.dma_start(wred_bf[:], wred_d), inc=SWQ, inc_by=16)

                    xs_done = [0] * NT; q_done = [0] * NT; tq_done = [0] * NT
                    qt_evac = [0] * NT; qn_done = [0] * NT; qcons_v = [0] * NT
                    hs_done = [0] * NT; ssq_done = [0] * NT

                    for t in range(NT):
                        a = t % 2
                        xw = [(SX, 64)]
                        xws = list(xw)
                        if t >= 1:
                            xws.append((VP, ssq_done[t - 1]))
                        ln_sp, ssq_done[t] = ln_stats(B1, x_sb, t * D, xln_st[a][:], 1e-5, w_scalar=xws, w_vector=xw)
                        w = [(SP, ln_sp), (SSM, C.s_small)]
                        if t >= 2:
                            w.append((SP, xs_done[t - 2]))
                        for k in range(ND):
                            emit(B1, "tensor",
                                 (lambda aa, kk: (lambda e: e.transpose(psT[aa][:, kk * 128:(kk + 1) * 128],
                                                                        xln_st[aa][:, kk * 128:(kk + 1) * 128], ident[:])))(a, k),
                                 waits=(w if k == 0 else ()), inc=(TP if k == ND - 1 else None))
                        C.tp += 1
                        tx_tp = C.tp
                        C.sp += 1
                        emit(B1, "scalar", (lambda aa: (lambda e: e.copy(xt0_st[aa][:], psT[aa][:])))(a),
                             waits=[(TP, tx_tp)], inc=SP)
                        xs_done[t] = C.sp
                        w = [(SP, xs_done[t]), (SWQ, 32)]
                        if t >= 2:
                            w.append((VP, qcons_v[t - 2]))
                        for k in range(ND):
                            for hlf in range(2):
                                last = (k == ND - 1 and hlf == 1)
                                emit(B1, "tensor",
                                     (lambda aa, kk, hh, st_, so_: (lambda e: e.matmul(
                                         psQ[aa][:, hh * 512:(hh + 1) * 512],
                                         xt0_st[aa][:, kk * 128:(kk + 1) * 128],
                                         WqT[:, kk * D + hh * 512: kk * D + (hh + 1) * 512],
                                         start=st_, stop=so_)))(a, k, hlf, k == 0, k == ND - 1),
                                     waits=(w if (k == 0 and hlf == 0) else ()),
                                     inc=(TP if last else None))
                        C.tp += 1
                        q_done[t] = C.tp
                        wq2 = [(TP, q_done[t])]
                        if t >= 1:
                            wq2.append((VP, hs_done[t - 1]))
                        C.sp += 1
                        emit(B1, "scalar", (lambda aa: (lambda e: e.activation(qw2[:], psQ[aa][:], AF.Square)))(a),
                             waits=wq2, inc=SP)
                        sp_q2 = C.sp
                        C.vp += 1
                        emit(B1, "vector", lambda e: e.reduce_sum(col(S_SS2), qw2[:], axis=AX.X),
                             waits=[(SP, sp_q2)], inc=VP)
                        v_ss2 = C.vp
                        C.vp += 1
                        emit(B1, "vector", lambda e: e.tensor_scalar(col(S_SD2), col(S_SS2), 1.0 / D, 1e-6, ALU.mult, ALU.add),
                             waits=[(VP, v_ss2)], inc=VP)
                        vp_sd2p = C.vp
                        C.sp += 1
                        emit(B1, "scalar", lambda e: e.activation(col(S_SD2B), col(S_SD2), AF.Sqrt),
                             waits=[(VP, vp_sd2p)], inc=SP)
                        sp_sd2 = C.sp
                        C.vp += 1
                        emit(B1, "vector", (lambda aa: (lambda e: e.tensor_mul(qw_sb[:], psQ[aa][:], qnw[:])))(a),
                             waits=[(TP, q_done[t]), (SSM, C.s_small)], inc=VP)
                        v_qw = C.vp
                        C.vp += 1
                        emit(B1, "vector", lambda e: e.reciprocal(col(S_RS2), col(S_SD2B)), waits=[(SP, sp_sd2)], inc=VP)
                        v_rs2 = C.vp
                        C.vp += 1
                        emit(B1, "vector", lambda e: e.tensor_mul(qw2[:], qw_sb[:], qw_sb[:]),
                             waits=[(VP, v_qw), (SP, sp_q2)], inc=VP)
                        v_qw2 = C.vp
                        for h in range(NH):
                            C.vp += 1
                            emit(B1, "vector",
                                 (lambda hh: (lambda e: e.reduce_sum(h4[:, hh:hh + 1], qw2[:, hh * DH:(hh + 1) * DH], axis=AX.X)))(h),
                                 waits=([(VP, v_qw2)] if h == 0 else ()), inc=VP)
                        vp_hs = C.vp
                        hs_done[t] = C.vp
                        C.sp += 1
                        emit(B1, "scalar", lambda e: e.activation(h4[:, 4:8], h4[:, 0:4], AF.Sqrt),
                             waits=[(VP, vp_hs)], inc=SP)
                        sp_hsr = C.sp
                        C.vp += 1
                        emit(B1, "vector", lambda e: e.tensor_scalar(h4[:, 8:12], h4[:, 4:8], col(S_RS2), 1e-5, ALU.mult, ALU.add),
                             waits=[(SP, sp_hsr), (VP, v_rs2)], inc=VP)
                        v_t4 = C.vp
                        C.vp += 1
                        emit(B1, "vector", lambda e: e.reciprocal(h4[:, 12:16], h4[:, 8:12]),
                             waits=[(VP, v_t4)], inc=VP)
                        v_t5 = C.vp
                        C.vp += 1
                        emit(B1, "vector", lambda e: e.tensor_scalar(h4[:, 12:16], h4[:, 12:16], col(S_RS2), 0.0, ALU.mult, ALU.add),
                             waits=[(VP, v_t5)], inc=VP)
                        qcons_v[t] = C.vp
                        vp_fs = C.vp
                        for h in range(NH):
                            C.sp += 1
                            emit(B1, "scalar",
                                 (lambda aa, hh: (lambda e: e.activation(
                                     qn_st[aa][:, hh * DH:(hh + 1) * DH], qw_sb[:, hh * DH:(hh + 1) * DH],
                                     AF.Identity, scale=h4[:, 12 + hh:13 + hh])))(a, h),
                                 waits=([(VP, vp_fs)] if h == 0 else ()), inc=SP)
                        qn_done[t] = C.sp
                        w = [(SP, qn_done[t])]
                        if t >= 2:
                            w.append((VP, qt_evac[t - 2]))
                        for k in range(ND):
                            emit(B1, "tensor",
                                 (lambda aa, kk: (lambda e: e.transpose(psT2[aa][:, kk * 128:(kk + 1) * 128],
                                                                        qn_st[aa][:, kk * 128:(kk + 1) * 128], ident[:])))(a, k),
                                 waits=(w if k == 0 else ()), inc=(TP if k == ND - 1 else None))
                        C.tp += 1
                        tq_done[t] = C.tp
                        C.vp += 1
                        emit(B1, "vector",
                             (lambda aa, tt: (lambda e: e.tensor_copy(
                                 QT[:].rearrange("p (i c) -> p i c", i=ND)[:, :, tt * 128:(tt + 1) * 128],
                                 psT2[aa][:].rearrange("p (i c) -> p i c", i=ND))))(a, t),
                             waits=[(TP, tq_done[t])], inc=VP)
                        qt_evac[t] = C.vp

                    b1_tp, b1_sp, b1_vp = C.tp, C.sp, C.vp

                with ExitStack() as s2:
                    e2 = s2.enter_context
                    w0T = e2(nc.sbuf_tensor("w0T", [128, 4096], BF16))
                    w2T = e2(nc.sbuf_tensor("w2T", [128, 4096], BF16))
                    w1T = e2(nc.sbuf_tensor("w1T", [128, 4096], BF16))
                    gq_st = [e2(nc.sbuf_tensor(f"gq{i}", [128, 512], F32)) for i in range(2)]
                    G_sb = e2(nc.sbuf_tensor("G_sb", [128, 4 * T], BF16))
                    psG = [e2(nc.psum_tensor(f"psG{i}", [128, 512], F32)) for i in range(2)]
                    psO = [e2(nc.psum_tensor(f"psO{i}", [128, 512], F32)) for i in range(2)]

                    B2 = new_block()
                    barA = [(TP, b1_tp), (SP, b1_sp), (VP, b1_vp)]
                    for i, (dst, src) in enumerate(((w0T, w0t_d), (w2T, w2t_d), (w1T, w1t_d))):
                        C.s_wh += 16
                        emit(B2, "gpsimd", (lambda d_, s_: (lambda e: e.dma_start(d_[:], s_)))(dst, src),
                             waits=(barA if i == 0 else ()), inc=SWH, inc_by=16)

                    sil_done = {}; g_done = {}
                    oev_done = [0] * NT
                    for h in range(NH):
                        for jt in range(4):
                            for tb in range(4):
                                i = h * 16 + jt * 4 + tb
                                w = [(SWH, 48), (VP, qt_evac[NT - 1])]
                                if i >= 1:
                                    w.append((SP, sil_done[i - 1]))
                                for dhc in range(2):
                                    emit(B2, "tensor",
                                         (lambda hh, jj, bb, dd, st_, so_: (lambda e: e.matmul(
                                             psG[0][:],
                                             w0T[:, hh * 1024 + dd * 512 + jj * 128: hh * 1024 + dd * 512 + (jj + 1) * 128],
                                             QT[:, (2 * hh + dd) * T + bb * 512: (2 * hh + dd) * T + (bb + 1) * 512],
                                             start=st_, stop=so_)))(h, jt, tb, dhc, dhc == 0, dhc == 1),
                                         waits=(w if dhc == 0 else ()), inc=(TP if dhc == 1 else None))
                                C.tp += 1
                                tp_gq = C.tp
                                w2_ = []
                                if i >= 1:
                                    w2_.append((VP, g_done[i - 1]))
                                for dhc in range(2):
                                    emit(B2, "tensor",
                                         (lambda hh, jj, bb, dd, st_, so_: (lambda e: e.matmul(
                                             psG[1][:],
                                             w2T[:, hh * 1024 + dd * 512 + jj * 128: hh * 1024 + dd * 512 + (jj + 1) * 128],
                                             QT[:, (2 * hh + dd) * T + bb * 512: (2 * hh + dd) * T + (bb + 1) * 512],
                                             start=st_, stop=so_)))(h, jt, tb, dhc, dhc == 0, dhc == 1),
                                         waits=(w2_ if dhc == 0 else ()), inc=(TP if dhc == 1 else None))
                                C.tp += 1
                                tp_hq = C.tp
                                wsl = [(TP, tp_gq)]
                                if i >= 2:
                                    wsl.append((VP, g_done[i - 2]))
                                C.sp += 1
                                emit(B2, "scalar",
                                     (lambda ii: (lambda e: e.activation(gq_st[ii % 2][:], psG[0][:], AF.Silu)))(i),
                                     waits=wsl, inc=SP)
                                sil_done[i] = C.sp
                                C.vp += 1
                                emit(B2, "vector",
                                     (lambda jj, bb, ii: (lambda e: e.tensor_mul(
                                         G_sb[:, jj * T + bb * 512:jj * T + (bb + 1) * 512],
                                         psG[1][:], gq_st[ii % 2][:])))(jt, tb, i),
                                     waits=[(TP, tp_hq), (SP, sil_done[i])], inc=VP)
                                g_done[i] = C.vp
                        vp_gall = C.vp
                        for tt in range(NT):
                            w = [(VP, vp_gall)]
                            if tt >= 2:
                                w.append((VP, oev_done[tt - 2]))
                            for jc in range(4):
                                emit(B2, "tensor",
                                     (lambda hh, tt_, jj, st_, so_: (lambda e: e.matmul(
                                         psO[tt_ % 2][:, 0:256],
                                         G_sb[:, jj * T + tt_ * 128:jj * T + (tt_ + 1) * 128],
                                         w1T[:, hh * 1024 + jj * 256: hh * 1024 + (jj + 1) * 256],
                                         start=st_, stop=so_)))(h, tt, jc, jc == 0, jc == 3),
                                     waits=(w if jc == 0 else ()), inc=(TP if jc == 3 else None))
                            C.tp += 1
                            tp_o = C.tp
                            C.vp += 1
                            emit(B2, "vector",
                                 (lambda hh, tt_: (lambda e: e.tensor_add(
                                     x_sb[:, tt_ * D + hh * DH: tt_ * D + (hh + 1) * DH],
                                     psO[tt_ % 2][:, 0:256],
                                     x_sb[:, tt_ * D + hh * DH: tt_ * D + (hh + 1) * DH])))(h, tt),
                                 waits=[(TP, tp_o)], inc=VP)
                            oev_done[tt] = C.vp

                    if dbg:
                        emit(B2, "gpsimd", lambda e: e.dma_start(dbgQT_d, QT[:]),
                             waits=[(VP, qt_evac[NT - 1])], inc=SDBG, inc_by=16)
                    b2_tp, b2_sp, b2_vp = C.tp, C.sp, C.vp

            with ExitStack() as s3:
                e3 = s3.enter_context
                xln2_st = [e3(nc.sbuf_tensor(f"x2ln{i}", [128, D], BF16)) for i in range(2)]
                if SPARSE:
                    x1lnF = e3(nc.sbuf_tensor("x1lnF", [128, NT * D], BF16))
                    zbf = e3(nc.sbuf_tensor("zbf", [128, D], BF16))
                    rowT = [e3(nc.sbuf_tensor(f"rowT{i}", [8, T], F32)) for i in range(4)]
                    ecols = e3(nc.sbuf_tensor("ecols", [8, 4], F32))
                    zrow = e3(nc.sbuf_tensor("zrow", [1, T], F32))
                    islotF = e3(nc.sbuf_tensor("islotF", [128, 16], F32))
                redT = e3(nc.sbuf_tensor("redT", [16, T], BF16))
                wgn_pad = e3(nc.sbuf_tensor("wgn_pad", [128, 16], BF16))
                wgsc = e3(nc.sbuf_tensor("wgsc", [8, 20], F32))
                psT3 = [e3(nc.psum_tensor(f"psT3{i}", [128, 1024], BF16)) for i in range(2)]
                psR = e3(nc.psum_tensor("psR", [16, T], F32))
                psL = [e3(nc.psum_tensor(f"psL{i}", [128, 512], F32)) for i in range(2)]

                B3 = new_block()
                barB = [(TP, b2_tp), (SP, b2_sp), (VP, b2_vp)]

                C.sp += 1
                emit(B3, "scalar", lambda e: e.activation(wgsc[:, 0:16], wg_sb[:], AF.Square),
                     waits=barB + [(SSM, C.s_small)], inc=SP)
                sp_wg0 = C.sp
                C.vp += 1
                emit(B3, "vector", lambda e: e.reduce_sum(wgsc[:, 16:17], wgsc[:, 0:16], axis=AX.X),
                     waits=barB + [(SP, sp_wg0)], inc=VP)
                vp_wg1 = C.vp
                C.sp += 1
                emit(B3, "scalar", lambda e: e.activation(wgsc[:, 17:18], wgsc[:, 16:17], AF.Sqrt),
                     waits=[(VP, vp_wg1)], inc=SP)
                sp_wg1 = C.sp
                C.vp += 1
                emit(B3, "vector", lambda e: e.tensor_scalar(wgsc[:, 18:19], wgsc[:, 17:18], 1.0, 1e-4, ALU.mult, ALU.max),
                     waits=[(SP, sp_wg1)], inc=VP)
                v_wg3 = C.vp
                C.sp += 1
                emit(B3, "scalar", lambda e: e.activation(wgsc[:, 19:20], wgsc[:, 18:19], AF.Identity),
                     waits=[(VP, v_wg3)], inc=SP)
                sp_wg4 = C.sp
                C.vp += 1
                emit(B3, "vector", lambda e: e.reciprocal(wgsc[:, 16:17], wgsc[:, 19:20]),
                     waits=[(SP, sp_wg4)], inc=VP)
                v_wg5 = C.vp
                C.vp += 1
                emit(B3, "vector", lambda e: e.memset(wgn_pad[:], 0.0), waits=barB, inc=VP)
                vp_wgall = C.vp
                C.sp += 1
                emit(B3, "scalar", lambda e: e.activation(wgn_pad[0:8, :], wg_sb[:], AF.Identity, scale=wgsc[:, 16:17]),
                     waits=[(VP, vp_wgall), (VP, v_wg5)], inc=SP)
                sp_wgn = C.sp
                emit(B3, "tensor", lambda e: e.transpose(psT3[0][0:16, 0:128], wgn_pad[:], ident[:]),
                     waits=barB + [(SP, sp_wgn), (SSM, C.s_small)], inc=TP)
                C.tp += 1
                tp_wgt = C.tp
                C.sp += 1
                emit(B3, "scalar", lambda e: e.copy(wgnT[:], psT3[0][0:16, 0:8]), waits=[(TP, tp_wgt)], inc=SP)
                sp_wgnT = C.sp

                x1t_done = [0] * NT
                ssq2_done = [0] * NT
                for t in range(NT):
                    a = t % 2
                    extra = barB if t == 0 else []
                    xws = list(extra)
                    if t >= 1:
                        xws.append((VP, ssq2_done[t - 1]))
                    dstln = x1lnF[:, t * D:(t + 1) * D] if SPARSE else xln2_st[a][:]
                    ln_sp, ssq2_done[t] = ln_stats(B3, x_sb, t * D, dstln, 1e-5,
                                     w_scalar=xws, w_vector=list(extra))
                    w = [(SP, ln_sp)]
                    if t == 0:
                        w.append((SP, sp_wgnT))
                    if t >= 2:
                        w.append((VP, x1t_done[t - 2]))
                    for k in range(ND):
                        srcln = ((lambda tt, kk: x1lnF[:, tt * D + kk * 128: tt * D + (kk + 1) * 128])(t, k)
                                 if SPARSE else xln2_st[a][:, k * 128:(k + 1) * 128])
                        emit(B3, "tensor",
                             (lambda aa, kk, sl: (lambda e: e.transpose(psT3[aa][:, kk * 128:(kk + 1) * 128],
                                                                        sl, ident[:])))(a, k, srcln),
                             waits=(w if k == 0 else ()), inc=(TP if k == ND - 1 else None))
                    C.tp += 1
                    tpv = C.tp
                    C.vp += 1
                    emit(B3, "vector",
                         (lambda aa, tt: (lambda e: e.tensor_copy(
                             X1T[:].rearrange("p (i c) -> p i c", i=ND)[:, :, tt * 128:(tt + 1) * 128],
                             psT3[aa][:].rearrange("p (i c) -> p i c", i=ND))))(a, t),
                         waits=[(TP, tpv)], inc=VP)
                    x1t_done[t] = C.vp
                    ln2_sp_last = ln_sp
                vp_x1t_all = C.vp

                for k in range(ND):
                    for tb in range(4):
                        last = (k == ND - 1 and tb == 3)
                        emit(B3, "tensor",
                             (lambda kk, bb, st_, so_: (lambda e: e.matmul(
                                 psR[:, bb * 512:(bb + 1) * 512],
                                 wred_bf[:, kk * 16:(kk + 1) * 16],
                                 X1T[:, kk * T + bb * 512: kk * T + (bb + 1) * 512],
                                 start=st_, stop=so_)))(k, tb, k == 0, k == ND - 1),
                             waits=([(VP, vp_x1t_all), (SWQ, 32)] if (k == 0 and tb == 0) else ()),
                             inc=(TP if last else None))
                C.tp += 1
                tp_red = C.tp
                C.sp += 1
                emit(B3, "scalar", lambda e: e.copy(redT[:], psR[:]), waits=[(TP, tp_red)], inc=SP)
                sp_redbf = C.sp

                lg_done = [0] * NT
                d8_done = [0] * NT
                for t in range(NT):
                    a = t % 2
                    w = [(SP, sp_redbf)]
                    if t >= 2:
                        w.append((SP, lg_done[t - 2]))
                    emit(B3, "tensor",
                         (lambda aa, tt: (lambda e: e.matmul(
                             psL[aa][:, 0:8], redT[:, tt * 128:(tt + 1) * 128], wgnT[:],
                             start=True, stop=True)))(a, t),
                         waits=w, inc=TP)
                    C.tp += 1
                    tp_lg = C.tp
                    wlg = [(TP, tp_lg)]
                    if t >= 2:
                        wlg.append((VP, d8_done[t - 2]))
                    C.sp += 1
                    emit(B3, "scalar", (lambda aa: (lambda e: e.copy(lgS[:, aa * 8:aa * 8 + 8], psL[aa][:, 0:8])))(a),
                         waits=wlg, inc=SP)
                    lg_done[t] = C.sp
                    lgsl = lgS[:, a * 8:a * 8 + 8]
                    C.vp += 1
                    emit(B3, "vector", (lambda ls: (lambda e: e.reduce_max(col(S_NMX), ls, axis=AX.X, negate=True)))(lgsl),
                         waits=[(SP, lg_done[t])], inc=VP)
                    v_nmx = C.vp
                    d8_done[t] = C.vp
                    if SPARSE:
                        C.vp += 1
                        emit(B3, "vector", (lambda ls: (lambda e: e.max(mxi8[:], ls)))(lgsl),
                             waits=[(SP, lg_done[t])], inc=VP)
                        v_mx8 = C.vp
                        C.vp += 1
                        emit(B3, "vector", (lambda ls: (lambda e: e.max_index(idx8u[:], mxi8[:], ls)))(lgsl),
                             waits=[(VP, v_mx8)], inc=VP)
                        v_i8 = C.vp
                        C.vp += 1
                        emit(B3, "vector", (lambda tt: (lambda e: e.tensor_copy(idxf[:, tt:tt + 1], idx8u[:, 0:1])))(t),
                             waits=[(VP, v_i8)], inc=VP)
                        d8_done[t] = C.vp
                    else:
                        C.sp += 1
                        emit(B3, "scalar", (lambda ls: (lambda e: e.activation(routS[:, 0:8], ls, AF.Identity, bias=col(S_NMX))))(lgsl),
                             waits=[(VP, v_nmx), (SP, lg_done[t])], inc=SP)
                        sp_d8 = C.sp
                        C.sp += 1
                        emit(B3, "scalar", lambda e: e.sign(routS[:, 8:16], routS[:, 0:8]),
                             waits=[(SP, sp_d8)], inc=SP)
                        sp_sg = C.sp
                    C.sp += 1
                    emit(B3, "scalar", (lambda ls: (lambda e: e.activation(routS[:, 16:24], ls, AF.Exp, bias=col(S_NMX))))(lgsl),
                         waits=[(VP, v_nmx), (SP, lg_done[t])], inc=SP)
                    sp_sx = C.sp
                    C.vp += 1
                    emit(B3, "vector", lambda e: e.reduce_sum(col(S_ESUM), routS[:, 16:24], axis=AX.X),
                         waits=[(SP, sp_sx)], inc=VP)
                    v_es = C.vp
                    C.sp += 1
                    emit(B3, "scalar", lambda e: e.activation(col(S_REC), col(S_ESUM), AF.Identity),
                         waits=[(VP, v_es)], inc=SP)
                    sp_esc = C.sp
                    C.vp += 1
                    emit(B3, "vector", lambda e: e.reciprocal(col(S_REC2), col(S_REC)),
                         waits=[(SP, sp_esc)], inc=VP)
                    v_rec = C.vp
                    if SPARSE:
                        C.sp += 1
                        emit(B3, "scalar", (lambda tt: (lambda e: e.activation(
                            scoreS[:, tt:tt + 1], col(S_REC2), AF.Identity)))(t),
                            waits=[(VP, v_rec)], inc=SP)
                    else:
                        C.sp += 1
                        emit(B3, "scalar", (lambda tt: (lambda e: e.activation(
                            c_sb[:, tt * E:(tt + 1) * E], routS[:, 8:16], AF.Identity,
                            scale=col(S_REC2), bias=col(S_REC2))))(t),
                            waits=[(VP, v_rec), (SP, sp_sg)], inc=SP)
                b3_c_sp = C.sp
                b3_c_vp = C.vp

                if SPARSE:
                    C.vp += 1
                    emit(B3, "vector", lambda e: e.memset(zbf[:], 0.0), waits=barB, inc=VP)
                    v_zbf = C.vp
                    for k in range(NSLOT // 128):
                        emit(B3, "gpsimd",
                             (lambda kk: (lambda e: e.dma_start(slotX_d[kk * 128:(kk + 1) * 128, :], zbf[:])))(k),
                             waits=([(VP, v_zbf)] if k == 0 else ()), inc=SZF, inc_by=16)
                    sr = 0
                    sr += 16
                    emit(B3, "gpsimd", lambda e: e.dma_start(
                        rowA_d.rearrange("o (p c) -> o p c", p=128), idxf[:]),
                        waits=[(VP, d8_done[NT - 1])], inc=SRP, inc_by=16)
                    for e_ in range(E):
                        sr += 16
                        emit(B3, "gpsimd", (lambda ee: (lambda e: e.dma_start(rowT[0][ee:ee + 1, :], rowA_d)))(e_),
                             waits=([(SRP, 16)] if e_ == 0 else ()), inc=SRP, inc_by=16)
                    sr_rep = sr
                    emit(B3, "gpsimd", lambda e: e.iota(ecols[:, 0:1], [[1, 1]], channel_multiplier=1,
                                                        allow_small_or_imprecise_dtypes=True), inc=GPC)
                    emit(B3, "gpsimd", lambda e: e.tensor_scalar(ecols[:, 1:2], ecols[:, 0:1], -1.0, 0.0, ALU.mult, ALU.add),
                         waits=[(GPC, 1)], inc=GPC)
                    emit(B3, "gpsimd", lambda e: e.tensor_scalar(ecols[:, 2:3], ecols[:, 0:1], float(CAP), -1.0, ALU.mult, ALU.add),
                         waits=[(GPC, 2)], inc=GPC)
                    C.vp += 1
                    emit(B3, "vector", lambda e: e.tensor_scalar(rowT[1][:], rowT[0][:], 1.0, ecols[:, 1:2], ALU.mult, ALU.add),
                         waits=[(SRP, sr_rep), (GPC, 3)], inc=VP)
                    v1 = C.vp
                    C.sp += 1
                    emit(B3, "scalar", lambda e: e.sign(rowT[2][:], rowT[1][:]), waits=[(SP, C.sp - 1), (VP, v1)], inc=SP)
                    s1 = C.sp
                    C.sp += 1
                    emit(B3, "scalar", lambda e: e.activation(rowT[1][:], rowT[2][:], AF.Square),
                         waits=[(SP, s1)], inc=SP)
                    s2 = C.sp
                    C.vp += 1
                    emit(B3, "vector", lambda e: e.tensor_scalar(rowT[2][:], rowT[1][:], -1.0, 1.0, ALU.mult, ALU.add),
                         waits=[(SP, s2)], inc=VP)
                    v2 = C.vp
                    C.vp += 1
                    emit(B3, "vector", lambda e: e.tensor_tensor_scan(rowT[1][:], rowT[2][:], rowT[2][:], 0.0, ALU.add, ALU.bypass),
                         waits=[(VP, v2)], inc=VP)
                    v3 = C.vp
                    C.vp += 1
                    emit(B3, "vector", lambda e: e.tensor_scalar(rowT[3][:], rowT[1][:], 1.0, float(CAP), ALU.mult, ALU.min),
                         waits=[(VP, v3)], inc=VP)
                    v4 = C.vp
                    C.vp += 1
                    emit(B3, "vector", lambda e: e.tensor_scalar(rowT[1][:], rowT[3][:], 1.0, ecols[:, 2:3], ALU.mult, ALU.add),
                         waits=[(VP, v4)], inc=VP)
                    v5 = C.vp
                    C.vp += 1
                    emit(B3, "vector", lambda e: e.tensor_mul(rowT[3][:], rowT[1][:], rowT[2][:]),
                         waits=[(VP, v5)], inc=VP)
                    v6 = C.vp
                    C.vp += 1
                    emit(B3, "vector", lambda e: e.memset(zrow[:], 0.0), inc=VP)
                    v7 = C.vp
                    sr += 16
                    emit(B3, "gpsimd", lambda e: e.dma_start(rowB_d, zrow[:]),
                         waits=[(VP, v7)], inc=SRP, inc_by=16)
                    for e_ in range(E):
                        sr += 16
                        emit(B3, "gpsimd", (lambda ee: (lambda e: e.dma_start(rowB_d, rowT[3][ee:ee + 1, :], accum_op=ALU.add)))(e_),
                             waits=[(SRP, sr - 16)] + ([(VP, v6)] if e_ == 0 else []), inc=SRP, inc_by=16)
                    sr += 16
                    emit(B3, "gpsimd", lambda e: e.dma_start(islotF[:], rowB_d.rearrange("o (p c) -> o p c", p=128)),
                         waits=[(SRP, sr - 16)], inc=SRP, inc_by=16)
                    C.vp += 1
                    emit(B3, "vector", lambda e: e.tensor_copy(islotI[:], islotF[:]),
                         waits=[(SRP, sr)], inc=VP)
                    v_islot = C.vp
                    for tt in range(NT):
                        w = [(VP, v_islot)]
                        if tt == 0:
                            w.append((SP, ln2_sp_last))
                            w.append((SZF, 16 * (NSLOT // 128)))
                        emit(B3, "gpsimd",
                             (lambda t_: (lambda e: e.indirect_dma_start(
                                 slotX_d, bass.IndirectOffsetOnAxis(ap=islotI[:, t_:t_ + 1], axis=0),
                                 x1lnF[:, t_ * D:(t_ + 1) * D], None)))(tt),
                             waits=w, inc=SSC, inc_by=16)

                for g in range(4):
                    dst = out_d[g * 512:(g + 1) * 512, :].rearrange("(n p) d -> p n d", p=128)
                    src = x_sb[:, g * 4096:(g + 1) * 4096].rearrange("p (n d) -> p n d", n=4)
                    C.s_base += 16
                    emit(B3, "sync", (lambda d_, s_: (lambda e: e.dma_start(d_, s_)))(dst, src),
                         waits=(barB if g == 0 else ()), inc=SB, inc_by=16)

                if dbg:
                    emit(B3, "gpsimd", lambda e: e.dma_start(dbgX1T_d, X1T[:]),
                         waits=[(VP, vp_x1t_all)], inc=SDBG, inc_by=16)
                    emit(B3, "gpsimd", lambda e: e.dma_start(dbgRED_d, redT[:]),
                         waits=[(SP, sp_redbf)], inc=SDBG, inc_by=16)
                    emit(B3, "gpsimd", lambda e: e.dma_start(dbgC_d, c_sb[:]),
                         waits=[(SP, b3_c_sp)], inc=SDBG, inc_by=16)
                if dbg & 2:
                    emit(B3, "gpsimd", lambda e: e.nop(),
                         waits=[(SB, C.s_base), (SDBG, 64)])
                b3_tp, b3_sp, b3_vp = C.tp, C.sp, C.vp

        if dbg & 2:
            check_deadlock()
            for bb in blocks:
                with nc.Block() as blk:
                    for eng_name in ENGS:
                        ops = bb[eng_name]
                        if not ops:
                            continue

                        def runner(ops_):
                            def _(eng):
                                for f in ops_:
                                    f(eng)
                            return _
                        getattr(blk, eng_name)(runner(ops))
            return nc
        if SPARSE:
            with ExitStack() as s4:
                e4 = s4.enter_context
                XTe = [e4(nc.sbuf_tensor(f"XTe{i}", [128, ND * CAP], BF16)) for i in range(2)]
                HTe = e4(nc.sbuf_tensor("HTe", [128, 32 * CAP], BF16))
                w1sl = [e4(nc.sbuf_tensor(f"w1sl{i}", [128, 4096], BF16)) for i in range(2)]
                w2sl = [e4(nc.sbuf_tensor(f"w2sl{i}", [128, 4 * 1024], BF16)) for i in range(2)]
                ysl = [e4(nc.sbuf_tensor(f"ysl{i}", [128, D], F32)) for i in range(2)]
                yg = [e4(nc.sbuf_tensor(f"yg{i}", [128, D], F32)) for i in range(2)]
                ytmp = [e4(nc.sbuf_tensor(f"ytmp{i}", [128, D], F32)) for i in range(2)]
                b2bf = e4(nc.sbuf_tensor("b2bf", [1, E * D], BF16))
                psY = [e4(nc.psum_tensor(f"psY{i}", [128, 1024], F32)) for i in range(4)]

                B4 = new_block()
                barC = [(TP, b3_tp), (SP, b3_sp), (VP, b3_vp), (SB, C.s_base), (SSC, 16 * NT)]

                w1_ready = {}; w2_ready = {}
                w1_consumed = {}; w2_consumed = {}
                slab_idx = 0
                for e_ in range(E):
                    for s_ in range(8):
                        slot = slab_idx % 2
                        C.s_w1[slot] += 32
                        w1_ready[(e_, s_)] = (slot, C.s_w1[slot])
                        slab_idx += 1
                w2_idx = 0
                for e_ in range(E):
                    for s_ in range(8):
                        slot = w2_idx % 2
                        C.s_w2[slot] += 16
                        w2_ready[(e_, s_)] = (slot, C.s_w2[slot])
                        w2_idx += 1

                gelu_hist = []
                gelu_done = {}
                xt_consumed = {}
                ysl_sp = {}
                ysl_sys = {}
                sys_cnt = [0, 0]
                sxt_cnt = [0, 0]
                xt_ready = {}
                for ee in range(E):
                    for cc in range(4):
                        ysl_sys[(ee, cc)] = 16 * (ee * 2 + cc // 2 + 1)
                for e_ in range(E):
                    wts = []
                    if e_ == 0:
                        wts = [(SSC, 16 * NT), (SB, C.s_base)]
                    if e_ >= 2:
                        wts.append((TP, xt_consumed[e_ - 2]))
                    first = True
                    for k in range(4):
                        for dc in range(ND):
                            sxt_cnt[e_ % 2] += 16
                            emit(B4, "sync",
                                 (lambda ee, kk, dd: (lambda e: e.dma_start(
                                     XTe[ee % 2][:, dd * CAP + kk * 128: dd * CAP + (kk + 1) * 128],
                                     slotX_d[ee * CAP + kk * 128: ee * CAP + (kk + 1) * 128,
                                             dd * 128:(dd + 1) * 128],
                                     transpose=True)))(e_, k, dc),
                                 waits=(wts if first else ()), inc=SXT[e_ % 2], inc_by=16)
                            first = False
                    xt_ready[e_] = sxt_cnt[e_ % 2]
                    if e_ >= 1:
                        pe = e_ - 1
                        for tc in range(4):
                            sys_cnt[tc % 2] += 16
                            assert ysl_sys[(pe, tc)] == sys_cnt[tc % 2]
                            emit(B4, "sync",
                                 (lambda ee, cc, rr: (lambda e: e.dma_start(
                                     slotY_d[ee * CAP + cc * 128: ee * CAP + (cc + 1) * 128, :],
                                     ysl[rr][:])))(pe, tc, tc % 2),
                                 waits=[(SP, ysl_sp[(pe, tc)])], inc=SYS[tc % 2], inc_by=16)

                    for fcl in range(32):
                        slot, w1c = w1_ready[(e_, fcl // 4)]
                        ph = psY[0][:, (fcl % 2) * 512:(fcl % 2) * 512 + 512]
                        w = []
                        if fcl % 4 == 0:
                            w.append((SW1[slot], w1c))
                        if fcl == 0:
                            w.append((SXT[e_ % 2], xt_ready[e_]))
                            if e_ == 0:
                                w += barC
                            else:
                                w.append((SP, ysl_sp[(e_ - 1, 1)]))
                        gi = len(gelu_hist)
                        if gi >= 2:
                            w.append((SP, gelu_hist[gi - 2]))
                        for dc in range(ND):
                            emit(B4, "tensor",
                                 (lambda ee, ff, dd, st_, so_, slt, pph: (lambda e: e.matmul(
                                     pph,
                                     w1sl[slt][:, dd * 512 + (ff % 4) * 128: dd * 512 + (ff % 4) * 128 + 128],
                                     XTe[ee % 2][:, dd * CAP: (dd + 1) * CAP],
                                     start=st_, stop=so_)))(e_, fcl, dc, dc == 0, dc == ND - 1, slot, ph),
                                 waits=(w if dc == 0 else ()), inc=(TP if dc == ND - 1 else None))
                        C.tp += 1
                        tp_g = C.tp
                        C.sp += 1
                        emit(B4, "scalar",
                             (lambda ee, ff, pph: (lambda e: e.activation(
                                 HTe[:, ff * CAP:(ff + 1) * CAP], pph, AF.Gelu,
                                 bias=b1c[:, ee * 32 + ff: ee * 32 + ff + 1])))(e_, fcl, ph),
                             waits=[(TP, tp_g)] + ([(SB2, 16)] if (e_ == 0 and fcl == 0) else []),
                             inc=SP)
                        gelu_hist.append(C.sp)
                        if fcl // 4 >= 2:
                            pass
                        w1_consumed[(e_, fcl // 4)] = C.tp
                    gelu_done[e_] = C.sp
                    xt_consumed[e_] = C.tp

                    for tc in range(4):
                        w = [(SP, gelu_done[e_])]
                        if e_ >= 1 and tc >= 2:
                            w.append((SP, ysl_sp[(e_ - 1, tc)]))
                        for dh in range(2):
                            emit(B4, "tensor",
                                 (lambda cc, ee, dd: (lambda e: e.matmul(
                                     psY[cc][:, dd * 512:(dd + 1) * 512],
                                     ones1[0:1, :], b2bf[0:1, ee * D + dd * 512: ee * D + (dd + 1) * 512],
                                     start=True, stop=False)))(tc, e_, dh),
                                 waits=(w if dh == 0 else ()))
                    tc_tp = {}
                    for fcl in range(32):
                        slot, w2c = w2_ready[(e_, fcl // 4)]
                        for tc in range(4):
                            w = []
                            if fcl % 4 == 0 and tc == 0:
                                w.append((SW2[slot], w2c))
                            for dh in range(2):
                                emit(B4, "tensor",
                                     (lambda cc, ff, dd, slt: (lambda e: e.matmul(
                                         psY[cc][:, dd * 512:(dd + 1) * 512],
                                         HTe[:, ff * CAP + cc * 128: ff * CAP + cc * 128 + 128],
                                         w2sl[slt][:, (ff % 4) * 1024 + dd * 512: (ff % 4) * 1024 + (dd + 1) * 512],
                                         start=False, stop=(ff == 31))))(tc, fcl, dh, slot),
                                     waits=(w if dh == 0 else ()),
                                     inc=(TP if dh == 1 else None))
                            C.tp += 1
                            if fcl == 31:
                                tc_tp[tc] = C.tp
                        if fcl % 4 == 3:
                            w2_consumed[(e_, fcl // 4)] = C.tp
                    for tc in range(4):
                        w = [(TP, tc_tp[tc])]
                        k = e_ * 4 + tc
                        if k >= 2:
                            pk = ((k - 2) // 4, (k - 2) % 4)
                            if pk in ysl_sys:
                                w.append((SYS[pk[1] % 2], ysl_sys[pk]))
                        C.sp += 1
                        emit(B4, "scalar",
                             (lambda cc, rr: (lambda e: e.activation(ysl[rr][:], psY[cc][:], AF.Identity)))(tc, tc % 2),
                             waits=w, inc=SP)
                        ysl_sp[(e_, tc)] = C.sp

                for tc in range(4):
                    sys_cnt[tc % 2] += 16
                    assert ysl_sys[(E - 1, tc)] == sys_cnt[tc % 2]
                    emit(B4, "sync",
                         (lambda ee, cc, rr: (lambda e: e.dma_start(
                             slotY_d[ee * CAP + cc * 128: ee * CAP + (cc + 1) * 128, :],
                             ysl[rr][:])))(E - 1, tc, tc % 2),
                         waits=[(SP, ysl_sp[(E - 1, tc)])], inc=SYS[tc % 2], inc_by=16)

                def emit_w1s(e_, s_):
                    slot, _ = w1_ready[(e_, s_)]
                    gidx = e_ * 8 + s_
                    w = []
                    if gidx >= 2:
                        pe, ps = divmod(gidx - 2, 8)
                        w.append((TP, w1_consumed[(pe, ps)]))
                    fc0 = s_ * 512
                    for g in range(2):
                        src = w1_d[e_, g * 512:(g + 1) * 512, fc0:fc0 + 512].rearrange("(c p) f -> p c f", p=128)
                        dst = w1sl[slot][:, g * 2048:(g + 1) * 2048].rearrange("p (c f) -> p c f", c=4)
                        emit(B4, "gpsimd", (lambda d_, s__: (lambda e: e.dma_start(d_, s__)))(dst, src),
                             waits=(w if g == 0 else ()), inc=SW1[slot], inc_by=16)

                def emit_w2s(e_, s_):
                    slot, _ = w2_ready[(e_, s_)]
                    gidx = e_ * 8 + s_
                    w = []
                    if gidx >= 2:
                        pe, ps = divmod(gidx - 2, 8)
                        w.append((TP, w2_consumed[(pe, ps)]))
                    src = w2_d[e_, s_ * 512:(s_ + 1) * 512, :].rearrange("(c p) d -> p c d", p=128)
                    dst = w2sl[slot][:].rearrange("p (c d) -> p c d", c=4)
                    emit(B4, "gpsimd", (lambda d_, s__: (lambda e: e.dma_start(d_, s__)))(dst, src),
                         waits=w, inc=SW2[slot], inc_by=16)

                emit(B4, "gpsimd", lambda e: e.dma_start(b2bf[:], b2r_d), waits=barC, inc=SB2, inc_by=16)
                for e_ in range(E):
                    for s_ in range(8):
                        emit_w1s(e_, s_)
                    for s_ in range(8):
                        emit_w2s(e_, s_)

                acc_k = [0, 0]
                sga_cnt = [0, 0]
                ytmp_sp = {}
                for tt in range(NT):
                    r = tt % 2
                    w = [(SYS[0], sys_cnt[0]), (SYS[1], sys_cnt[1])]
                    if tt >= 2:
                        w.append((SP, ytmp_sp[tt - 2]))
                    sga_cnt[r] += 16
                    emit(B4, "gpsimd",
                         (lambda t_, rr: (lambda e: e.indirect_dma_start(
                             yg[rr][:], None,
                             slotY_d, bass.IndirectOffsetOnAxis(ap=islotI[:, t_:t_ + 1], axis=0))))(tt, r),
                         waits=w, inc=SGA[r], inc_by=16)
                    par = tt % 2
                    wy = [(SGA[r], sga_cnt[r])]
                    if acc_k[par] >= 1:
                        wy.append((SAC[par], 16 * acc_k[par]))
                    C.sp += 1
                    emit(B4, "scalar",
                         (lambda t_, rr: (lambda e: e.activation(
                             ytmp[rr][:], yg[rr][:], AF.Identity, scale=scoreS[:, t_:t_ + 1])))(tt, r),
                         waits=wy, inc=SP)
                    ytmp_sp[tt] = C.sp
                    wa = [(SP, ytmp_sp[tt])]
                    if tt == 0:
                        wa.append((SB, C.s_base))
                    emit(B4, "gpsimd",
                         (lambda t_, rr: (lambda e: e.dma_start(
                             out_d[t_ * 128:(t_ + 1) * 128, :], ytmp[rr][:], accum_op=ALU.add)))(tt, r),
                         waits=wa, inc=SAC[par], inc_by=16)
                    acc_k[par] += 1

                emit(B4, "gpsimd", lambda e: e.nop(),
                     waits=[(SAC[0], 16 * acc_k[0]), (SAC[1], 16 * acc_k[1])] +
                           ([(SDBG, 64)] if dbg else []))

                check_deadlock()
                for bb in blocks:
                    with nc.Block() as blk:
                        for eng_name in ENGS:
                            ops = bb[eng_name]
                            if not ops:
                                continue

                            def runner(ops_):
                                def _(eng):
                                    for f in ops_:
                                        f(eng)
                                return _
                            getattr(blk, eng_name)(runner(ops))
            return nc

        with ExitStack() as s4:
            e4 = s4.enter_context
            HT = e4(nc.sbuf_tensor("HT", [128, NFC * T], BF16))
            w1sl = [e4(nc.sbuf_tensor(f"w1sl{i}", [128, 4096], BF16)) for i in range(2)]
            w2buf = [e4(nc.sbuf_tensor(f"w2buf{i}", [128, NFC * D], BF16)) for i in range(2)]
            ytmp = [e4(nc.sbuf_tensor(f"ytmp{i}", [128, D], F32)) for i in range(2)]
            b2bf = e4(nc.sbuf_tensor("b2bf", [1, E * D], BF16))
            psH = [e4(nc.psum_tensor(f"psH{i}", [128, 512], F32)) for i in range(2)]
            psY = [e4(nc.psum_tensor(f"psY{i}", [128, 1024], F32)) for i in range(2)]

            B4 = new_block()
            barC = [(TP, b3_tp), (SP, b3_sp), (VP, b3_vp), (SB, C.s_base)]

            units = [(e_, hf_) for e_ in range(E) for hf_ in range(2)]

            w1_slot_uses = {0: [], 1: []}
            w1_slab_ready = {}
            w2_ready = {}
            w1_consumed_tp = {}
            w2_consumed_tp = {}
            slab_idx = 0
            for ui, (e_, hf_) in enumerate(units):
                for s_ in range(4):
                    slot = slab_idx % 2
                    w1_slot_uses[slot].append((e_, hf_, s_))
                    C.s_w1[slot] += 32
                    w1_slab_ready[(e_, hf_, s_)] = (slot, C.s_w1[slot])
                    slab_idx += 1
                C.s_w2[ui % 2] += 64
                w2_ready[(e_, hf_)] = C.s_w2[ui % 2]

            gelu_hist = []
            gelu_done = {}
            ytmp_done = {}
            acc_k = [0, 0]
            accums = []

            for ui, (e_, hf_) in enumerate(units):
                for s_ in range(4):
                    slot, w1cnt = w1_slab_ready[(e_, hf_, s_)]
                    for ft in range(4):
                        fcl = s_ * 4 + ft
                        for tb in range(4):
                            gi = len(gelu_hist)
                            aa = gi % 2
                            w = []
                            if ft == 0 and tb == 0:
                                w.append((SW1[slot], w1cnt))
                                if ui == 0 and s_ == 0:
                                    w += barC
                            if gi >= 2:
                                w.append((SP, gelu_hist[gi - 2]))
                            for k in range(ND):
                                emit(B4, "tensor",
                                     (lambda aa_, slt, kk, ff, bb, st_, so_: (lambda e: e.matmul(
                                         psH[aa_][:],
                                         w1sl[slt][:, kk * 512 + ff * 128: kk * 512 + (ff + 1) * 128],
                                         X1T[:, kk * T + bb * 512: kk * T + (bb + 1) * 512],
                                         start=st_, stop=so_)))(aa, slot, k, ft, tb, k == 0, k == ND - 1),
                                     waits=(w if k == 0 else ()), inc=(TP if k == ND - 1 else None))
                            C.tp += 1
                            tp_g = C.tp
                            wg_ = [(TP, tp_g)]
                            if ui == 0 and fcl == 0 and tb == 0:
                                wg_.append((SB, C.s_base))
                            C.sp += 1
                            emit(B4, "scalar",
                                 (lambda aa_, ee, hh, ff, bb: (lambda e: e.activation(
                                     HT[:, ff * T + bb * 512: ff * T + (bb + 1) * 512],
                                     psH[aa_][:], AF.Gelu,
                                     bias=b1c[:, ee * 32 + hh * 16 + ff: ee * 32 + hh * 16 + ff + 1])))(aa, e_, hf_, fcl, tb),
                                 waits=wg_, inc=SP)
                            gelu_hist.append(C.sp)
                    w1_consumed_tp[(e_, hf_, s_)] = C.tp
                gelu_done[(e_, hf_)] = C.sp

                for tc in range(NT):
                    b = tc % 2
                    w = [(SP, gelu_done[(e_, hf_)]), (SW2[ui % 2], w2_ready[(e_, hf_)])]
                    key2 = (e_, hf_, tc - 2)
                    if key2 in ytmp_done:
                        w.append((SP, ytmp_done[key2]))
                    elif ui > 0:
                        pe, ph = units[ui - 1]
                        w.append((SP, ytmp_done[(pe, ph, NT - 2 + tc)]))
                    if hf_ == 0:
                        for dh in range(2):
                            emit(B4, "tensor",
                                 (lambda bb, ee, dd: (lambda e: e.matmul(
                                     psY[bb][:, dd * 512:(dd + 1) * 512],
                                     ones1[0:1, :], b2bf[0:1, ee * D + dd * 512: ee * D + (dd + 1) * 512],
                                     start=True, stop=False)))(b, e_, dh),
                                 waits=((w + [(SB2, 16)]) if dh == 0 else ()))
                    for fcl in range(NFC):
                        for dh in range(2):
                            last_all = (fcl == NFC - 1 and dh == 1)
                            emit(B4, "tensor",
                                 (lambda bb, ff, tt, dd, st__, so__, wb: (lambda e: e.matmul(
                                     psY[bb][:, dd * 512:(dd + 1) * 512],
                                     HT[:, ff * T + tt * 128: ff * T + (tt + 1) * 128],
                                     w2buf[wb][:, ff * D + dd * 512: ff * D + (dd + 1) * 512],
                                     start=st__, stop=so__)))(b, fcl, tc, dh,
                                                              (hf_ == 1 and fcl == 0), fcl == NFC - 1, ui % 2),
                                 waits=(w if (hf_ == 1 and fcl == 0 and dh == 0) else ()),
                                 inc=(TP if last_all else None))
                    C.tp += 1
                    tp_y = C.tp
                    par = tc % 2
                    kclass = acc_k[par]
                    wy = [(TP, tp_y), (SP, b3_c_sp)]
                    if kclass >= 1:
                        wy.append((SAC[par], 16 * kclass))
                    C.sp += 1
                    emit(B4, "scalar",
                         (lambda bb, ee, tt: (lambda e: e.activation(
                             ytmp[tt % 2][:], psY[bb][:], AF.Identity,
                             scale=c_sb[:, tt * E + ee: tt * E + ee + 1])))(b, e_, tc),
                         waits=wy, inc=SP)
                    ytmp_done[(e_, hf_, tc)] = C.sp
                    acc_k[par] += 1
                    accums.append((e_, hf_, tc, par, acc_k[par], C.sp))
                w2_consumed_tp[(e_, hf_)] = C.tp

            def emit_w1(e_, hf_, s_, slot, first_waits=()):
                uses = w1_slot_uses[slot]
                myi = uses.index((e_, hf_, s_))
                w = list(first_waits)
                if myi >= 1:
                    w.append((TP, w1_consumed_tp[uses[myi - 1]]))
                fc0 = hf_ * FH + s_ * 512
                for g in range(2):
                    src = w1_d[e_, g * 512:(g + 1) * 512, fc0:fc0 + 512].rearrange("(c p) f -> p c f", p=128)
                    dst = w1sl[slot][:, g * 2048:(g + 1) * 2048].rearrange("p (c f) -> p c f", c=4)
                    emit(B4, "gpsimd", (lambda d_, s__: (lambda e: e.dma_start(d_, s__)))(dst, src),
                         waits=(w if g == 0 else ()), inc=SW1[slot], inc_by=16)

            def emit_w2(e_, hf_, r, first_waits=()):
                ui_ = e_ * 2 + hf_
                w = list(first_waits)
                if ui_ >= 2:
                    w.append((TP, w2_consumed_tp[units[ui_ - 2]]))
                for g in range(4):
                    src = w2_d[e_, hf_ * FH + g * 512: hf_ * FH + (g + 1) * 512, :].rearrange("(c p) d -> p c d", p=128)
                    dst = w2buf[r][:, g * 4096:(g + 1) * 4096].rearrange("p (c d) -> p c d", c=4)
                    emit(B4, "gpsimd", (lambda d_, s__: (lambda e: e.dma_start(d_, s__)))(dst, src),
                         waits=(w if g == 0 else ()), inc=SW2[r], inc_by=16)

            emit(B4, "gpsimd", lambda e: e.dma_start(b2bf[:], b2r_d), waits=barC, inc=SB2, inc_by=16)
            emit_w1(0, 0, 0, w1_slab_ready[(0, 0, 0)][0])
            emit_w1(0, 0, 1, w1_slab_ready[(0, 0, 1)][0])
            emit_w2(0, 0, 0)
            emit_w1(0, 0, 2, w1_slab_ready[(0, 0, 2)][0])
            emit_w1(0, 0, 3, w1_slab_ready[(0, 0, 3)][0])

            acc_by_unit = {}
            for a_ in accums:
                acc_by_unit.setdefault((a_[0], a_[1]), []).append(a_)

            def emit_acc(a_):
                ea, ha, tca, para, kcla, spva = a_
                w = [(SP, spva)]
                if ea == 0 and ha == 0 and tca == 0:
                    w.append((SB, C.s_base))
                emit(B4, "gpsimd",
                     (lambda tt: (lambda e: e.dma_start(
                         out_d[tt * 128:(tt + 1) * 128, :], ytmp[tt % 2][:], accum_op=ALU.add)))(tca),
                     waits=w, inc=SAC[para], inc_by=16)

            for ui, (e_, hf_) in enumerate(units):
                aops = acc_by_unit[(e_, hf_)]
                early = []
                late = []
                if ui + 1 < len(units):
                    ne, nh = units[ui + 1]
                    early = [("w1", (ne, nh, 0, w1_slab_ready[(ne, nh, 0)][0])),
                             ("w1", (ne, nh, 1, w1_slab_ready[(ne, nh, 1)][0])),
                             ("w2", (ne, nh, (ui + 1) % 2))]
                    late = [("w1", (ne, nh, 2, w1_slab_ready[(ne, nh, 2)][0])),
                            ("w1", (ne, nh, 3, w1_slab_ready[(ne, nh, 3)][0]))]
                for i, a_ in enumerate(aops):
                    if i < len(early):
                        kind, pl = early[i]
                        (emit_w1 if kind == "w1" else emit_w2)(*pl)
                    emit_acc(a_)
                for kind, pl in late:
                    (emit_w1 if kind == "w1" else emit_w2)(*pl)

            emit(B4, "gpsimd", lambda e: e.nop(),
                 waits=[(SAC[0], 16 * acc_k[0]), (SAC[1], 16 * acc_k[1])] +
                       ([(SDBG, 64)] if dbg else []))

            check_deadlock()

            for bb in blocks:
                with nc.Block() as blk:
                    for eng_name in ENGS:
                        ops = bb[eng_name]
                        if not ops:
                            continue

                        def runner(ops_):
                            def _(eng):
                                for f in ops_:
                                    f(eng)
                            return _
                        getattr(blk, eng_name)(runner(ops))
    return nc


def kernel(**inputs):
    x = np.asarray(inputs["L0_attn_hidden_states"], dtype=np.float32)
    qkv_w = np.asarray(inputs["L0_attn_qkv_w"], dtype=np.float32)
    q_norm_w = np.asarray(inputs["L0_attn_q_norm_w"], dtype=np.float32)
    w0 = np.asarray(inputs["L0_attn_w0"], dtype=np.float32)
    w1a = np.asarray(inputs["L0_attn_w1"], dtype=np.float32)
    w2a = np.asarray(inputs["L0_attn_w2"], dtype=np.float32)
    w_red = np.asarray(inputs["L0_ffn_w_red"], dtype=np.float32)
    wg = np.asarray(inputs["L0_ffn_wg"], dtype=np.float32)
    W1 = np.asarray(inputs["L0_ffn_w1"], dtype=np.float32)
    b1 = np.asarray(inputs["L0_ffn_b1"], dtype=np.float32)
    W2 = np.asarray(inputs["L0_ffn_w2"], dtype=np.float32)
    b2 = np.asarray(inputs["L0_ffn_b2"], dtype=np.float32)

    Wq = qkv_w[:D]
    wqt = np.ascontiguousarray(Wq.T.reshape(ND, 128, D).transpose(1, 0, 2).reshape(128, ND * D))
    w0t = np.ascontiguousarray(np.stack([w0[h].T.reshape(2, 128, J) for h in range(NH)])
                               .transpose(2, 0, 1, 3).reshape(128, 4096))
    w2t = np.ascontiguousarray(np.stack([w2a[h].T.reshape(2, 128, J) for h in range(NH)])
                               .transpose(2, 0, 1, 3).reshape(128, 4096))
    w1t = np.ascontiguousarray(np.stack([w1a[h].T.reshape(4, 128, DH) for h in range(NH)])
                               .transpose(2, 0, 1, 3).reshape(128, 4096))
    wredl = np.ascontiguousarray(w_red.reshape(ND, 128, 16).transpose(1, 0, 2).reshape(128, 128))
    qnw_rep = np.ascontiguousarray(np.broadcast_to(q_norm_w, (128, D)))
    b1c = np.ascontiguousarray(b1.reshape(E, 32, 128).transpose(2, 0, 1).reshape(128, 256))
    b2r = np.ascontiguousarray(b2.reshape(1, E * D))
    ident = np.eye(128, dtype=ml_dtypes.bfloat16)
    ones1 = np.ones((1, 128), dtype=ml_dtypes.bfloat16)

    nc = build_nc(dbg=_DBG[0])
    common = dict(wqt=wqt, w0t=w0t, w2t=w2t, w1t=w1t, wred=wredl, wg=wg, qnw=qnw_rep,
                  b1c=b1c, b2r=b2r, ident=ident, ones1=ones1, w1e=W1, w2e=W2)
    in_maps = [dict(common, x=np.ascontiguousarray(x[i])) for i in range(8)]
    res = run_bass_kernel_spmd(nc, in_maps, core_ids=list(range(8)))
    if _DBG[0]:
        _DBG[1] = res.results
    return np.stack([r["out"] for r in res.results]).astype(np.float32)


_DBG = [0, None]
